# version 08
# baseline (speedup 1.0000x reference)
import contextlib
import numpy as np
import ml_dtypes
import concourse.bass as bass
import concourse.mybir as mybir
from concourse.bass_utils import run_bass_kernel_spmd

F32 = mybir.dt.float32
BF16 = mybir.dt.bfloat16
U32 = mybir.dt.uint32
AF = mybir.ActivationFunctionType
ALU = mybir.AluOpType
AX = mybir.AxisListType
BFnp = ml_dtypes.bfloat16

B_PHASE = 2
B_QC = 99
SHARED_AG = True

D = 1024
T = 2048
BSEQ = 2
NTOK = BSEQ * T
NT = NTOK // 128
TPS = T // 128
DI = 2048
DS = 128
NH = 32
PH = 64
ZXB = 2 * DI + 2 * DS + NH
NHA = 16
HDA = 64
FFN = 4096
E = 16
FE = 1408
CAP = 5120
NTOT = 32768
NCORES = 8


def _recalc():
    global NTOT, CAP
    NTOT = NCORES * NTOK
    CAP = int(np.ceil(1.25 * NTOT * 2 / E))

LN_EPS = 1e-5
RMS_EPS = 1e-6
MN_EPS = 1e-5


class Ch:

    def __init__(self, stack, nc, name, step=1):
        self.sem = stack.enter_context(nc.semaphore(name))
        self.n = 0
        self.step = step

    def inc(self, instr):
        if isinstance(instr, int):
            return instr
        instr.then_inc(self.sem, self.step)
        self.n += self.step
        return self.n

    def wait(self, eng, n=None):
        v = self.n if n is None else n
        if v > 0:
            eng.wait_ge(self.sem, v)


_CHAINED_OPS = {
    "tensor_scalar", "tensor_tensor", "tensor_copy", "tensor_reduce", "reduce_max",
    "reduce_sum", "reciprocal", "memset", "memzero", "scalar_tensor_tensor",
    "tensor_tensor_scan", "select", "activation", "copy", "mul", "add", "sqrt",
    "square", "sign", "tensor_add", "tensor_sub", "tensor_mul", "tensor_max",
    "tensor_relu", "tensor_scalar_add", "tensor_scalar_mul", "tensor_scalar_max",
    "tensor_scalar_min", "tensor_scalar_sub", "tensor_single_scalar", "max",
    "max_with_indices", "iota", "bn_stats", "bn_aggr",
}


_PE_OPS = {"matmul", "transpose", "ldweights"}


class SelfChain:

    def __init__(self, eng, own, others, ops, self_wait=True):
        self._eng = eng
        self._own = own
        self._others = list(others)
        self._ops = ops
        self._self_wait = self_wait
        self._last = {}

    def _sync(self):
        chans = ([self._own] if (self._own is not None and self._self_wait) else []) + self._others
        for c in chans:
            if c.n > self._last.get(id(c), 0):
                self._eng.wait_ge(c.sem, c.n)
                self._last[id(c)] = c.n

    def reset(self):
        self._last = {}

    def __getattr__(self, name):
        f = getattr(self._eng, name)
        if name not in self._ops:
            return f

        def wrapped(*a, **kw):
            self._sync()
            instr = f(*a, **kw)
            if self._own is not None:
                self._own.inc(instr)
                return self._own.n
            return instr

        return wrapped


class DmaChan:

    def __init__(self, stack, nc, name, k=4, sync_chans=()):
        self.slots = [Ch(stack, nc, f"{name}{i}", 16) for i in range(k)]
        self.k = k
        self.i = 0
        self.sync_chans = list(sync_chans)
        self._last = {}

    def issue(self, eng, instr_fn):
        for c in self.sync_chans:
            if c.n > self._last.get(id(c), 0):
                eng.wait_ge(c.sem, c.n)
                self._last[id(c)] = c.n
        s = self.slots[self.i % self.k]
        self.i += 1
        if s.n > 0:
            eng.wait_ge(s.sem, s.n)
        s.inc(instr_fn())
        return (s, s.n)

    @staticmethod
    def wait_tok(eng, tok):
        s, n = tok
        eng.wait_ge(s.sem, n)

    def wait_all(self, eng):
        for s in self.slots:
            if s.n > 0:
                eng.wait_ge(s.sem, s.n)

    def reset(self):
        self.i = 0
        self._last = {}
        for s in self.slots:
            s.n = 0


def build_nc(dump=0):
    nc = bass.Bass("TRN2", target_bir_lowering=False)

    def inp(name, shape, dtype=F32):
        return nc.dram_tensor(name, shape, dtype, kind="ExternalInput").ap()

    x_in = inp("x", [NTOK, D])
    w_in = inp("W_in", [D, ZXB])
    conv_w = inp("conv_w", [DI + 2 * DS, 4])
    conv_b = inp("conv_b", [DI + 2 * DS, 1])
    a_log = inp("A_log", [NH, 1])
    dm_exp = inp("Dm_exp", [128, DI // 128])
    dt_bias = inp("dt_bias", [NH, 1])
    mnorm_w = inp("mnorm_w", [128, DI // 128])
    w_out = inp("W_out", [DI, D])
    w_qkv = inp("W_qkv", [D, D + 2 * HDA])
    w_cproj = inp("W_cproj", [D, D])
    qconv_w = inp("qconv_w", [128, 3])
    qconv_b = inp("qconv_b", [128, 1])
    kconv_w = inp("kconv_w", [HDA, 3])
    kconv_b = inp("kconv_b", [HDA, 1])
    vconv_w = inp("vconv_w", [HDA, 3])
    vconv_b = inp("vconv_b", [HDA, 1])
    maa_k = inp("time_maa_k", [128, D // 128])
    maa_r = inp("time_maa_r", [128, D // 128])
    w_key = inp("W_key", [D, FFN])
    w_rec = inp("W_rec", [D, D])
    w_val = inp("W_val", [FFN, D])
    router_w = inp("router_w", [D, E])
    rb_bcast = inp("rb_bcast", [128, E])
    my_wg = inp("my_wg", [2, D, FE])
    my_wu = inp("my_wu", [2, D, FE])
    my_wd = inp("my_wd", [2, FE, D])
    sw_gate = inp("sw_gate", [D, FE])
    sw_up = inp("sw_up", [D, FE])
    sw_down = inp("sw_down", [FE, D])
    ident_bf = inp("ident_bf", [128, 128], BF16)
    ident_f32 = inp("ident_f32", [128, 128])
    ltri_strict = inp("ltri_strict", [128, 128])
    ltri32 = inp("ltri32", [32, 32])
    negmask = inp("negmask", [128, 8, 128])
    trimask_bf = inp("trimask_bf", [128, 128], BF16)
    iota16 = inp("iota16", [128, E])
    prevmask32 = inp("prevmask32", [NCORES, 32])
    tid_f32 = inp("tid_f32", [128, NT])
    ones_row = inp("ones_row", [1, 128])
    sel32 = inp("sel32", [128, NT * 32])
    sel1col = inp("sel1col", [32, NT * 128])

    out_x = nc.dram_tensor("out", [NTOK, D], F32, kind="ExternalOutput").ap()
    dmp = None
    if dump:
        dmp = nc.dram_tensor("d_stage", [NTOK, D], F32, kind="ExternalOutput").ap()

    xres = nc.dram_tensor("xres", [NTOK, D], F32).ap()
    beta_dram = nc.dram_tensor("beta_dram", [2, NH, 128], F32).ap()
    bl_dram = nc.dram_tensor("bl_dram", [2, 1, NH], F32).ap()
    agh_in = nc.dram_tensor("agh_in", [NTOK, D], BF16).ap()
    agh_out = nc.dram_tensor("agh_out", [NTOT, D], BF16,
                             addr_space="Shared" if SHARED_AG else "Local").ap()
    cnt_in = nc.dram_tensor("cnt_in", [1, E], F32).ap()
    cnt_out = nc.dram_tensor("cnt_out", [NCORES, E], F32).ap()
    slot_in = nc.dram_tensor("slot_in", [E * CAP, 1], F32).ap()
    rs_out = nc.dram_tensor("rs_out", [2 * CAP, 1], F32).ap()
    agy_in = nc.dram_tensor("agy_in", [2 * CAP, D], BF16).ap()
    agy_out = nc.dram_tensor("agy_out", [NCORES * 2 * CAP, D], BF16,
                             addr_space="Shared" if SHARED_AG else "Local").ap()
    shared_dram = nc.dram_tensor("shared_dram", [NTOK, D], BF16).ap()

    gp, sp = nc.gpsimd, nc.sync

    with contextlib.ExitStack() as st:
        ch = {
            "pe": Ch(st, nc, "c_pe"),
            "ve": Ch(st, nc, "c_ve"),
            "se": Ch(st, nc, "c_se"),
            "cc": Ch(st, nc, "c_cc"),
        }
        ve = SelfChain(nc.vector, ch["ve"], [ch["se"], ch["pe"]], _CHAINED_OPS)
        se = SelfChain(nc.scalar, ch["se"], [ch["ve"], ch["pe"]], _CHAINED_OPS)
        pe = SelfChain(nc.tensor, ch["pe"], [ch["ve"], ch["se"]], _PE_OPS, self_wait=False)
        comp = (ch["ve"], ch["se"], ch["pe"])
        spd = DmaChan(st, nc, "spd", 6, comp)
        gpd = DmaChan(st, nc, "gpd", 6, comp)
        sod = DmaChan(st, nc, "sod", 3, comp)
        semnums = ([c.sem.num for c in ch.values()]
                   + [sl.sem.num for d in (spd, gpd, sod) for sl in d.slots])
        lo, hi = min(semnums), max(semnums)

        def stage_boundary():
            nc.all_engine_barrier()

        nc.all_engine_barrier()
        gp.sem_clear(range(lo, hi + 1))
        nc.all_engine_barrier()

        const_bf = st.enter_context(nc.sbuf_tensor("k_ident_bf", [128, 128], BF16))
        identf_sb = st.enter_context(nc.sbuf_tensor("k_ident_f", [128, 128], F32))
        trim_sb = st.enter_context(nc.sbuf_tensor("k_trim", [128, 128], BF16))
        ltri_sb = st.enter_context(nc.sbuf_tensor("k_ltri", [128, 128], F32))
        ltri32_sb = st.enter_context(nc.sbuf_tensor("k_ltri32", [32, 32], F32))
        iota16_sb = st.enter_context(nc.sbuf_tensor("k_iota16", [128, E], F32))
        pm32_sb = st.enter_context(nc.sbuf_tensor("k_pm32", [NCORES, 32], F32))
        tid_sb = st.enter_context(nc.sbuf_tensor("k_tid", [128, NT], F32))
        ones_sb = st.enter_context(nc.sbuf_tensor("k_ones", [1, 128], F32))
        onescol_bf = st.enter_context(nc.sbuf_tensor("k_onescol", [128, 1], BF16))
        rb_sb = st.enter_context(nc.sbuf_tensor("k_rb", [128, E], F32))

        for dst, src in [(const_bf, ident_bf), (identf_sb, ident_f32), (trim_sb, trimask_bf),
                         (ltri_sb, ltri_strict), (ltri32_sb, ltri32), (iota16_sb, iota16),
                         (pm32_sb, prevmask32), (tid_sb, tid_f32), (ones_sb, ones_row),
                         (rb_sb, rb_bcast)]:
            spd.issue(sp, lambda dst=dst, src=src: sp.dma_start(out=dst[:], in_=src[:, :]))
        ve.memset(onescol_bf[:], 1.0)

        def ln_tile(xt, h1, sq, stats, eps, sub_mean, rr_out=None):
            s1, s2, s3 = stats[:, 0:1], stats[:, 1:2], stats[:, 2:3]
            ch["se"].inc(se.activation(sq[:], xt[:], AF.Square, accum_out=s1))
            if sub_mean:
                ve.tensor_reduce(s2, xt[:], axis=AX.X, op=ALU.add)
                ve.tensor_scalar(s2, s2, 1.0 / D, None, ALU.mult)
                ve.tensor_tensor(s3, s2, s2, ALU.mult)
                ch["se"].wait(ve)
                ve.tensor_scalar(s1, s1, 1.0 / D, s3, ALU.mult, ALU.subtract)
                ch["ve"].inc(ve.tensor_scalar(s1, s1, eps, None, ALU.add))
            else:
                ch["se"].wait(ve)
                ch["ve"].inc(ve.tensor_scalar(s1, s1, 1.0 / D, eps, ALU.mult, ALU.add))
            ch["ve"].wait(se)
            ch["se"].inc(se.activation(s1, s1, AF.Sqrt))
            ch["se"].wait(ve)
            ve.reciprocal(s1, s1)
            if rr_out is not None:
                ve.tensor_copy(rr_out, s1)
            if sub_mean:
                return ch["ve"].inc(ve.tensor_scalar(h1[:], xt[:], s2, s1, ALU.subtract, ALU.mult))
            return ch["ve"].inc(ve.tensor_scalar(h1[:], xt[:], s1, None, ALU.mult))

        with contextlib.ExitStack() as sa:
            t = lambda name, shape, dtype=F32: sa.enter_context(nc.sbuf_tensor(name, shape, dtype))
            win_sb = t("win_sb", [128, 8, ZXB], BF16)
            wout_sb = t("wout_sb", [128, 16, D], BF16)
            cw_sb = t("cw_sb", [128, 18, 4])
            cb_sb = t("cb_sb", [128, 18])
            dtb_sb = t("dtb_sb", [NH, 1])
            nega_sb = t("nega_sb", [NH, 1])
            dme_sb = t("dme_sb", [128, 16])
            mnw_sb = t("mnw_sb", [128, 16])
            ones32 = t("ones32", [NH, 128])
            xt2 = t("xt2", [128, 2, D])
            h1 = t("h1A", [128, D])
            sq = t("sqA", [128, D])
            stats = t("statsA", [128, 4])
            hb = t("hbA", [128, D], BF16)
            h_fm = t("h_fmA", [128, 8, 128], BF16)
            z_sil = t("z_sil", [128, 16, 128], BF16)
            xbc_raw = t("xbc_raw", [128, 18, 131])
            xc_bf = t("xc_bf", [128, 18, 128], BF16)
            acc_a = t("acc_a", [128, 9, 128])
            acc_b = t("acc_b", [128, 9, 128])
            dt_raw = t("dt_raw", [NH, 128])
            dt_t0 = t("dt_t0", [NH, 128])
            dt_ab = t("dt_ab", [NH, 128])
            dt_sp2 = t("dt_sp2", [NH, 128])
            beta = t("beta", [NH, 128])
            gamma = t("gamma", [NH, 128])
            gT = t("gT", [128, NH])
            xm_tok = t("xm_tok", [128, 16, 128], BF16)
            b_tok = t("b_tok", [128, 128], BF16)
            bct = t("bct", [128, 128], BF16)
            negm_sb = t("negm_sb", [128, 8, 128])
            brep_all = t("brep_all", [128, 16, 128])
            argt = t("argA", [128, 8, 128])
            sp1 = t("sp1A", [128, 16, 128], BF16)
            sp2 = t("sp2A", [128, 16, 128], BF16)
            ebl = t("ebl", [128, NH])
            wtile = t("wtile", [128, NH])
            xw_all = t("xw_all", [128, 8, PH], BF16)
            t4_bf = t("t4_bf", [128, 4, 128], BF16)
            wt_bf = t("wt_bf", [128, NH], BF16)
            ebl_bf = t("ebl_bf", [128, NH], BF16)
            dme_bf = t("dme_bf", [128, 16], BF16)
            S0 = t("S0", [128, NH, PH], BF16)
            S1 = t("S1", [128, NH, PH], BF16)
            y_fm = t("y_fmA", [128, 16, 128], BF16)
            rstd_row = t("rstd_row", [1, 128])
            rstd_col = t("rstd_col", [128, 1])
            xnew = t("xnewA", [128, D])
            zt_bf = t("zt_bf", [128, 2, 512], BF16)
            dtf32 = t("dtf32", [128, 32])

            ps_zt = sa.enter_context(nc.psum_tensor("ps_zx", [128, 2, 512], F32))
            ps_out = sa.enter_context(nc.psum_tensor("ps_outA", [128, 2, 512], F32))
            ps_y = sa.enter_context(nc.psum_tensor("ps_yA", [128, 4, 128], F32))
            ps_S = sa.enter_context(nc.psum_tensor("ps_SA", [128, 8, PH], F32))
            ps_tr = sa.enter_context(nc.psum_tensor("ps_trA", [128, 8, 128], BF16))
            ps_m = sa.enter_context(nc.psum_tensor("ps_mA", [128, 128], F32))

            for k in range(8):
                gpd.issue(gp, lambda k=k: gp.dma_start(out=win_sb[:, k, :],
                                                       in_=w_in[k * 128:(k + 1) * 128, :]))
            for k in range(16):
                gpd.issue(gp, lambda k=k: gp.dma_start(out=wout_sb[:, k, :],
                                                       in_=w_out[k * 128:(k + 1) * 128, :]))
            for fc in range(18):
                spd.issue(sp, lambda fc=fc: sp.dma_start(out=cw_sb[:, fc, :],
                                                         in_=conv_w[fc * 128:(fc + 1) * 128, :]))
                spd.issue(sp, lambda fc=fc: sp.dma_start(out=cb_sb[:, fc:fc + 1],
                                                         in_=conv_b[fc * 128:(fc + 1) * 128, :]))
            spd.issue(sp, lambda: sp.dma_start(out=dtb_sb[:], in_=dt_bias[:, :]))
            spd.issue(sp, lambda: sp.dma_start(out=nega_sb[:], in_=a_log[:, :]))
            spd.issue(sp, lambda: sp.dma_start(out=dme_sb[:], in_=dm_exp[:, :]))
            spd.issue(sp, lambda: sp.dma_start(out=mnw_sb[:], in_=mnorm_w[:, :]))
            spd.issue(sp, lambda: sp.dma_start(out=negm_sb[:], in_=negmask[:, :, :]))
            spd.wait_all(ve)
            gpd.wait_all(ve)
            for k in range(16):
                ve.tensor_scalar(wout_sb[:, k, :], wout_sb[:, k, :], mnw_sb[:, k:k + 1], None, ALU.mult)
            ve.tensor_copy(dme_bf[:], dme_sb[:])
            ch["ve"].inc(ve.memset(ones32[:], 1.0))
            ch["ve"].wait(se)
            se.activation(nega_sb[:], nega_sb[:], AF.Exp)
            seA0 = ch["se"].inc(se.mul(nega_sb[:], nega_sb[:], -1.0))
            ch["ve"].inc(ve.memset(xbc_raw[:, :, 0:3], 0.0))

            xload = []
            xload.append(spd.issue(sp, lambda: sp.dma_start(out=xt2[:, 0, :], in_=x_in[0:128, :])))
            sadd_evac = [0] * 8
            xstore_tok = None
            yslot_evac = [0, 0, 0, 0]
            xnew_prev = 0

            for n in range(NT):
                seq, cc_ = divmod(n, TPS)
                buf = n % 2
                xtile = xt2[:, buf, :]
                par = n % 2
                if n + 1 < NT:
                    ch["ve"].wait(sp, xnew_prev)
                    xload.append(spd.issue(sp, lambda n=n: sp.dma_start(
                        out=xt2[:, (n + 1) % 2, :],
                        in_=x_in[(n + 1) * 128:(n + 2) * 128, :])))
                DmaChan.wait_tok(ve, xload[n])
                DmaChan.wait_tok(se, xload[n])

                ln_tile(xtile, h1, sq, stats, LN_EPS, True)
                ch["se"].inc(se.activation(sq[:], h1[:], AF.Square, accum_out=stats[:, 0:1]))
                ch["se"].wait(ve)
                ch["ve"].inc(ve.tensor_scalar(stats[:, 0:1], stats[:, 0:1], 1.0 / D, RMS_EPS,
                                              ALU.mult, ALU.add))
                ch["ve"].wait(se)
                ch["se"].inc(se.activation(stats[:, 0:1], stats[:, 0:1], AF.Sqrt))
                ch["se"].wait(ve)
                ve.reciprocal(stats[:, 0:1], stats[:, 0:1])
                ch["ve"].inc(ve.tensor_scalar(hb[:], h1[:], stats[:, 0:1], None, ALU.mult))
                hb_done = ch["ve"].n

                ch["ve"].wait(pe, hb_done)
                for k in range(8):
                    ch["pe"].inc(pe.transpose(ps_tr[:, k, :], hb[:, k * 128:(k + 1) * 128], const_bf[:]))
                ch["pe"].wait(ve)
                ch["ve"].inc(ve.tensor_copy(h_fm[:], ps_tr[:]))
                hfm_done = ch["ve"].n

                ch["ve"].wait(pe, hfm_done)
                for fg in range(9):
                    w = 512 if fg < 8 else 288
                    dst = ps_zt[:, fg % 2, 0:w]
                    for k in range(8):
                        pe.matmul(dst, h_fm[:, k, :], win_sb[:, k, fg * 512:fg * 512 + w],
                                  start=(k == 0), stop=(k == 7))
                    ve.tensor_copy(zt_bf[:, fg % 2, 0:w], dst)
                    base = (fg % 2) * 4
                    for j in range(4 if fg < 8 else 2):
                        pe.transpose(ps_tr[:, base + j, :],
                                     zt_bf[:, fg % 2, j * 128:(j + 1) * 128], const_bf[:])
                    if fg < 4:
                        se.activation(z_sil[:, fg * 4:(fg + 1) * 4, :],
                                      ps_tr[:, base:base + 4, :], AF.Silu)
                    elif fg < 8:
                        ve.tensor_copy(xbc_raw[:, fg * 4 - 16:fg * 4 - 12, 3:131],
                                       ps_tr[:, base:base + 4, :])
                    else:
                        ve.tensor_copy(xbc_raw[:, 16:18, 3:131], ps_tr[:, base:base + 2, :])
                        ve.tensor_copy(dtf32[:], ps_zt[:, fg % 2, 256:288])
                        pe.transpose(ps_m[0:NH, :], dtf32[:], identf_sb[:])
                        ve.tensor_copy(dt_raw[:], ps_m[0:NH, :])

                for h0 in (0, 9):
                    cw_b = lambda k: cw_sb[:, h0:h0 + 9, k:k + 1].broadcast_to([128, 9, 128])
                    xs = lambda k: xbc_raw[:, h0:h0 + 9, k:k + 128]
                    ve.tensor_tensor(acc_a[:], xs(0), cw_b(0), ALU.mult)
                    ve.tensor_tensor(acc_b[:], xs(1), cw_b(1), ALU.mult)
                    ve.tensor_tensor(acc_a[:], acc_a[:], acc_b[:], ALU.add)
                    ve.tensor_tensor(acc_b[:], xs(2), cw_b(2), ALU.mult)
                    ve.tensor_tensor(acc_a[:], acc_a[:], acc_b[:], ALU.add)
                    ve.tensor_tensor(acc_b[:], xs(3), cw_b(3), ALU.mult)
                    ve.tensor_tensor(acc_a[:], acc_a[:], acc_b[:], ALU.add)
                    ch["ve"].inc(ve.tensor_tensor(
                        acc_a[:], acc_a[:],
                        cb_sb[:, h0:h0 + 9].unsqueeze(2).broadcast_to([128, 9, 128]),
                        ALU.add))
                    ch["ve"].wait(se)
                    ch["se"].inc(se.activation(xc_bf[:, h0:h0 + 9, :], acc_a[:], AF.Silu))
                conv_done = ch["se"].n
                if cc_ == TPS - 1:
                    ch["ve"].inc(ve.memset(xbc_raw[:, :, 0:3], 0.0))
                else:
                    ch["ve"].inc(ve.tensor_copy(xbc_raw[:, :, 0:3], xbc_raw[:, :, 128:131]))

                ve.tensor_scalar(dt_t0[:], dt_raw[:], dtb_sb[:], None, ALU.add)
                ch["ve"].inc(ve.tensor_scalar(dt_ab[:], dt_t0[:], 1.0, None, ALU.mult))
                ch["ve"].wait(se)
                se.activation(dt_ab[:], dt_ab[:], AF.Abs)
                se.activation(dt_ab[:], dt_ab[:], AF.Exp, scale=-1.0)
                ch["se"].inc(se.activation(dt_ab[:], dt_ab[:], AF.Ln, bias=1.0))
                ch["se"].wait(ve, seA0)
                ch["se"].wait(ve)
                ve.tensor_scalar(dt_sp2[:], dt_t0[:], 0.0, None, ALU.max)
                ve.tensor_tensor(dt_sp2[:], dt_sp2[:], dt_ab[:], ALU.add)
                ve.tensor_scalar(dt_t0[:], dt_sp2[:], nega_sb[:], None, ALU.mult)
                ch["ve"].inc(ve.tensor_tensor_scan(beta[:], ones32[:], dt_t0[:], 0.0,
                                                   ALU.mult, ALU.add))
                beta_done = ch["ve"].n
                ch["ve"].wait(se)
                ch["se"].inc(se.activation(dt_ab[:], dt_sp2[:], AF.Ln))
                ch["se"].wait(ve)
                ch["ve"].inc(ve.tensor_tensor(gamma[:], beta[:], dt_ab[:], ALU.subtract))
                gamma_done = ch["ve"].n

                ch["ve"].wait(gp, beta_done)
                ch["se"].wait(gp)
                tok_bd = gpd.issue(gp, lambda: gp.dma_start(out=beta_dram[par, :, :], in_=beta[:, :]))
                tok_bl = gpd.issue(gp, lambda: gp.dma_start(out=bl_dram[par, :, :],
                                                            in_=beta[:, 127:128],
                                                            allow_slow_non_contiguous=True))
                DmaChan.wait_tok(gp, tok_bl)
                tok_blrep = gpd.issue(gp, lambda: gp.dma_start(
                    out=wtile[:],
                    in_=bl_dram[par, :, :].partition_broadcast(128)))
                DmaChan.wait_tok(se, tok_blrep)
                ch["se"].inc(se.activation(ebl[:], wtile[:], AF.Exp))
                ebl_done = ch["se"].n
                ve.tensor_copy(ebl_bf[:], ebl[:])
                DmaChan.wait_tok(gp, tok_bd)
                tok_ba = gpd.issue(gp, lambda: gp.dma_start(
                    out=brep_all[:].rearrange("p a b -> p (a b)"),
                    in_=beta_dram[par, 0:16, :].flatten().unsqueeze(0)
                        .partition_broadcast(128)))

                ch["ve"].wait(pe, gamma_done)
                ch["pe"].inc(pe.transpose(ps_m[:, 0:NH], gamma[:], identf_sb[0:NH, 0:NH]))
                ch["pe"].wait(ve)
                ch["ve"].inc(ve.tensor_copy(gT[:], ps_m[:, 0:NH]))
                ch["se"].wait(ve, ebl_done)
                DmaChan.wait_tok(ve, tok_blrep)
                ch["ve"].inc(ve.tensor_tensor(wtile[:], wtile[:], gT[:], ALU.subtract))
                ch["ve"].wait(se)
                ch["se"].inc(se.activation(wtile[:], wtile[:], AF.Exp))
                wtile_done = ch["se"].n
                ch["se"].wait(ve, wtile_done)
                ve.tensor_copy(wt_bf[:], wtile[:])

                ch["se"].wait(pe, conv_done)
                for base in range(0, 17, 8):
                    cnt = min(8, 17 - base)
                    for j in range(cnt):
                        ch["pe"].inc(pe.transpose(ps_tr[:, j, :], xc_bf[:, base + j, :],
                                                  const_bf[:]))
                    ch["pe"].wait(ve)
                    for j in range(cnt):
                        fc = base + j
                        dst2 = xm_tok[:, fc, :] if fc < 16 else b_tok[:]
                        ch["ve"].inc(ve.tensor_copy(dst2, ps_tr[:, j, :]))
                xmtok_done = ch["ve"].n

                ch["pe"].inc(pe.matmul(ps_m[:, 0:128], xc_bf[:, 16, :], xc_bf[:, 17, :],
                                       start=True, stop=True))
                ch["pe"].wait(ve)
                ch["ve"].inc(ve.tensor_copy(bct[:], ps_m[:, 0:128]))

                DmaChan.wait_tok(ve, tok_ba)
                DmaChan.wait_tok(se, tok_ba)
                for g in range(4):
                    e1t = sp1[:, 0:8, :]
                    g1 = sp1[:, 8:16, :]
                    e2b = sp2[:, 0:8, :]
                    g2 = sp2[:, 8:16, :]
                    if g == 2:
                        DmaChan.wait_tok(ve, tok_ba)
                        DmaChan.wait_tok(se, tok_ba)
                    ba_g = brep_all[:, (g % 2) * 8:(g % 2) * 8 + 8, :]
                    gt_rep = gT[:, g * 8:(g + 1) * 8].unsqueeze(2).broadcast_to([128, 8, 128])
                    ve.tensor_tensor(argt[:], ba_g, gt_rep, ALU.subtract)
                    ch["ve"].inc(ve.tensor_tensor(argt[:], argt[:], negm_sb[:], ALU.add))
                    ch["ve"].wait(se)
                    ch["se"].inc(se.activation(e1t, argt[:], AF.Exp))
                    ch["se"].inc(se.activation(e2b, ba_g, AF.Exp))
                    if g == 1:
                        tok_ba = gpd.issue(gp, lambda: gp.dma_start(
                            out=brep_all[:].rearrange("p a b -> p (a b)"),
                            in_=beta_dram[par, 16:32, :].flatten().unsqueeze(0)
                                .partition_broadcast(128)))
                    ch["se"].wait(ve)
                    bct_rep = bct[:, :].unsqueeze(1).broadcast_to([128, 8, 128])
                    ve.tensor_tensor(g1, e1t, bct_rep, ALU.mult)
                    c_rep = xc_bf[:, 17, :].unsqueeze(1).broadcast_to([128, 8, 128])
                    ch["ve"].inc(ve.tensor_tensor(g2, e2b, c_rep, ALU.mult))
                    g_done = ch["ve"].n

                    ch["ve"].wait(pe, g_done)
                    srcSg = (S0 if (n % 2 == 0) else S1)[:, g * 8:(g + 1) * 8, :]
                    dstSg = (S1 if (n % 2 == 0) else S0)[:, g * 8:(g + 1) * 8, :]
                    xm_g = xm_tok[:, g * 4:(g + 1) * 4, :].rearrange(
                        "p a (b c) -> p (a b) c", b=2)
                    ch["ve"].inc(ve.tensor_tensor(
                        xw_all[:], xm_g,
                        wt_bf[:, g * 8:(g + 1) * 8].unsqueeze(2).broadcast_to([128, 8, 64]),
                        ALU.mult))
                    ch["ve"].wait(pe)
                    for pr in range(4):
                        fcx = g * 4 + pr
                        for half in range(2):
                            h = 2 * fcx + half
                            hh = h - g * 8
                            lhs_xm = xm_tok[:, fcx, half * 64:half * 64 + 64]
                            tp = None if half == 0 else (0, 64)
                            if cc_ == 0:
                                ch["pe"].inc(pe.matmul(ps_y[half * 64:half * 64 + 64, pr, :],
                                                       lhs_xm, sp1[:, 8 + hh, :],
                                                       start=True, stop=True,
                                                       tile_position=tp))
                            else:
                                ch["pe"].inc(pe.matmul(ps_y[half * 64:half * 64 + 64, pr, :],
                                                       lhs_xm, sp1[:, 8 + hh, :],
                                                       start=True, stop=False,
                                                       tile_position=tp))
                                ch["pe"].inc(pe.matmul(ps_y[half * 64:half * 64 + 64, pr, :],
                                                       srcSg[:, hh, :], sp2[:, 8 + hh, :],
                                                       start=False, stop=True,
                                                       tile_position=tp))
                            ch["pe"].inc(pe.matmul(ps_S[:, hh, :], b_tok[:],
                                                   xw_all[:, hh, :],
                                                   start=True, stop=True))
                    allmm = ch["pe"].n
                    ch["pe"].wait(ve, allmm)
                    ch["se"].wait(ve, ebl_done)
                    if cc_ == 0:
                        ch["ve"].inc(ve.tensor_copy(dstSg, ps_S[:, 0:8, :]))
                    else:
                        ve.tensor_tensor(dstSg, srcSg,
                                         ebl_bf[:, g * 8:(g + 1) * 8].unsqueeze(2)
                                         .broadcast_to([128, 8, 64]), ALU.mult)
                        ch["ve"].inc(ve.tensor_tensor(dstSg, dstSg, ps_S[:, 0:8, :],
                                                      ALU.add))
                    ch["se"].wait(ve, conv_done)
                    ve.tensor_tensor(t4_bf[:], xc_bf[:, g * 4:(g + 1) * 4, :],
                                     dme_bf[:, g * 4:(g + 1) * 4].unsqueeze(2)
                                     .broadcast_to([128, 4, 128]), ALU.mult)
                    ch["ve"].inc(ve.tensor_tensor(y_fm[:, g * 4:(g + 1) * 4, :],
                                                  t4_bf[:], ps_y[:, :, :], ALU.add))

                ygb = sp1[:, :, :]
                ysq = sp2[:, :, :]
                ch["se"].wait(ve)
                ch["pe"].wait(ve)
                ve.tensor_tensor(ygb, y_fm[:], z_sil[:], ALU.mult)
                ch["ve"].inc(ve.tensor_tensor(ysq, ygb, ygb, ALU.mult))
                ysq_done = ch["ve"].n
                ch["ve"].wait(pe, ysq_done)
                for fcx in range(16):
                    ch["pe"].inc(pe.matmul(ps_m[0:1, 0:128], onescol_bf[:], sp2[:, fcx, :],
                                           start=(fcx == 0), stop=(fcx == 15)))
                ch["pe"].wait(ve)
                ve.tensor_scalar(rstd_row[:], ps_m[0:1, 0:128], 1.0 / DI, MN_EPS,
                                 ALU.mult, ALU.add)
                ch["ve"].inc(ve.tensor_scalar(rstd_row[:], rstd_row[:], 1.0, None, ALU.mult))
                ch["ve"].wait(se)
                ch["se"].inc(se.activation(rstd_row[:], rstd_row[:], AF.Sqrt))
                ch["se"].wait(ve)
                ch["ve"].inc(ve.reciprocal(rstd_row[:], rstd_row[:]))
                ch["ve"].wait(pe)
                ch["pe"].inc(pe.transpose(ps_m[:, 0:1], rstd_row[:], identf_sb[0:1, 0:1]))
                ch["pe"].wait(ve)
                ch["ve"].inc(ve.tensor_copy(rstd_col[:], ps_m[:, 0:1]))

                for half in range(2):
                    for k in range(16):
                        ch["pe"].inc(pe.matmul(ps_out[:, half, :], sp1[:, k, :],
                                               wout_sb[:, k, half * 512:(half + 1) * 512],
                                               start=(k == 0), stop=(k == 15)))
                ch["pe"].wait(ve)
                if xstore_tok is not None:
                    DmaChan.wait_tok(nc.vector, xstore_tok)
                ve.scalar_tensor_tensor(xnew[:, 0:512], ps_out[:, 0, :], rstd_col[:],
                                        h1[:, 0:512], ALU.mult, ALU.add)
                ch["ve"].inc(ve.scalar_tensor_tensor(xnew[:, 512:1024], ps_out[:, 1, :],
                                                     rstd_col[:], h1[:, 512:1024],
                                                     ALU.mult, ALU.add))
                xnew_prev = ch["ve"].n
                ch["ve"].wait(sp, xnew_prev)
                xstore_tok = spd.issue(sp, lambda n=n: sp.dma_start(
                    out=xres[n * 128:(n + 1) * 128, :], in_=xnew[:]))

            spd.wait_all(sp)

        stage_boundary()

        if (dump == 0 or dump >= 2) and B_PHASE > 0:
          with contextlib.ExitStack() as sb:
            t = lambda name, shape, dtype=F32: sb.enter_context(nc.sbuf_tensor(name, shape, dtype))
            wqkv_sb = t("wqkv_sb", [128, 8, D + 2 * HDA], BF16)
            wcp_sb = t("wcp_sb", [128, 8, D], BF16)
            qcw_sb = t("qcw_sb", [128, 3])
            qcb_sb = t("qcb_sb", [128, 1])
            kcw_sb = t("kcw_sb", [HDA, 3])
            kcb_sb = t("kcb_sb", [HDA, 1])
            vcw_sb = t("vcw_sb", [128, 3])
            vcb_sb = t("vcb_sb", [128, 1])
            xt2 = t("xt2B", [128, 2, D])
            h1 = t("h1B", [128, D])
            sq = t("sqB", [128, D])
            stats = t("statsB", [128, 4])
            hb = t("hbB", [128, D], BF16)
            h_fm = t("h_fmB", [128, 8, 128], BF16)
            q_raw = t("q_raw", [128, 8, T + 2], BF16)
            q_c = t("q_c", [128, 8, T], BF16)
            k_raw = t("k_raw", [HDA, T + 2], BF16)
            k_c = t("k_c", [HDA, T], BF16)
            k_dup = t("k_dup", [128, T], BF16)
            v_raw = t("v_raw", [128, T + 2], BF16)
            v_c = t("v_c", [128, T], BF16)
            v_tok = t("v_tok", [128, TPS, 72], BF16)
            acc_a = t("accBa", [128, 1024])
            acc_b = t("accBb", [128, 1024])
            pt = t("ptB", [128, 16, 128], BF16)
            y_acc = t("y_accB", [128, 16, 72])
            rec = t("recB", [128, 16])
            y_tok = t("y_tokB", [128, 16, 64], BF16)
            y_fmT = t("y_fmT", [128, 8, 128], BF16)
            xnew = t("xnewB", [128, D])

            for k in range(8):
                gpd.issue(gp, lambda k=k: gp.dma_start(out=wqkv_sb[:, k, :],
                                                       in_=w_qkv[k * 128:(k + 1) * 128, :]))
                gpd.issue(gp, lambda k=k: gp.dma_start(out=wcp_sb[:, k, :],
                                                       in_=w_cproj[k * 128:(k + 1) * 128, :]))
            spd.issue(sp, lambda: sp.dma_start(out=qcw_sb[:], in_=qconv_w[:, :]))
            spd.issue(sp, lambda: sp.dma_start(out=qcb_sb[:], in_=qconv_b[:, :]))
            spd.issue(sp, lambda: sp.dma_start(out=kcw_sb[:], in_=kconv_w[:, :]))
            spd.issue(sp, lambda: sp.dma_start(out=kcb_sb[:], in_=kconv_b[:, :]))
            spd.issue(sp, lambda: sp.dma_start(out=vcw_sb[64:128, :], in_=vconv_w[:, :]))
            spd.issue(sp, lambda: sp.dma_start(out=vcb_sb[64:128, :], in_=vconv_b[:, :]))
            spd.wait_all(nc.vector)
            spd.wait_all(nc.scalar)
            gpd.wait_all(nc.tensor)

            heads_eo = [0, 2, 4, 6, 8, 10, 12, 14, 1, 3, 5, 7, 9, 11, 13, 15]
            xstore_tok = None
            for s_ in range(BSEQ):
                with contextlib.ExitStack() as pp:
                    ps_pr = pp.enter_context(nc.psum_tensor(f"ps_prB{s_}", [128, 8, 128], F32))
                    ps_tr = pp.enter_context(nc.psum_tensor(f"ps_trB{s_}", [128, 8, 128], BF16))
                    pr_evac = [0] * 8
                    for cc_ in range(TPS):
                        n = s_ * TPS + cc_
                        tok = spd.issue(sp, lambda n=n: sp.dma_start(
                            out=xt2[:, n % 2, :], in_=xres[n * 128:(n + 1) * 128, :]))
                        DmaChan.wait_tok(nc.vector, tok)
                        DmaChan.wait_tok(nc.scalar, tok)
                        xtile = xt2[:, n % 2, :]
                        se.activation(sq[:], xtile, AF.Square, accum_out=stats[:, 0:1])
                        ve.tensor_scalar(stats[:, 0:1], stats[:, 0:1], 1.0 / D, RMS_EPS,
                                         ALU.mult, ALU.add)
                        se.activation(stats[:, 0:1], stats[:, 0:1], AF.Sqrt)
                        ve.reciprocal(stats[:, 0:1], stats[:, 0:1])
                        ve.tensor_scalar(hb[:], xtile, stats[:, 0:1], None, ALU.mult)
                        for k in range(8):
                            pe.transpose(ps_tr[:, k, :], hb[:, k * 128:(k + 1) * 128], const_bf[:])
                        ve.tensor_copy(h_fm[:], ps_tr[:])
                        for oc in range(9):
                            slot = oc % 8
                            dst = ps_pr[:, slot, :]
                            for k in range(8):
                                ch["pe"].inc(pe.matmul(dst, wqkv_sb[:, k, oc * 128:(oc + 1) * 128],
                                                       h_fm[:, k, :], start=(k == 0), stop=(k == 7)))
                            if oc < 8:
                                ve.tensor_copy(q_raw[:, oc, 2 + cc_ * 128:2 + (cc_ + 1) * 128], dst)
                            else:
                                ve.tensor_copy(k_raw[:, 2 + cc_ * 128:2 + (cc_ + 1) * 128],
                                               dst[0:64, :])
                                ve.tensor_copy(v_raw[64:128, 2 + cc_ * 128:2 + (cc_ + 1) * 128],
                                               dst[64:128, :])
                    ve.memset(q_raw[:, :, 0:2], 0.0)
                    ve.memset(k_raw[:, 0:2], 0.0)
                    ve.memset(v_raw[64:128, 0:2], 0.0)
                    HW_ = min(1024, T)
                    for fc in range(8):
                        for o in range(0, T, HW_):
                            ve.tensor_scalar(acc_a[:, 0:HW_], q_raw[:, fc, o:o + HW_],
                                             qcw_sb[:, 0:1], qcb_sb[:], ALU.mult, ALU.add)
                            ve.scalar_tensor_tensor(acc_b[:, 0:HW_], q_raw[:, fc, o + 1:o + 1 + HW_],
                                                    qcw_sb[:, 1:2], acc_a[:, 0:HW_], ALU.mult, ALU.add)
                            ve.scalar_tensor_tensor(q_c[:, fc, o:o + HW_],
                                                    q_raw[:, fc, o + 2:o + 2 + HW_],
                                                    qcw_sb[:, 2:3], acc_b[:, 0:HW_], ALU.mult, ALU.add)
                    for o in range(0, T, HW_):
                        ve.tensor_scalar(acc_a[0:64, 0:HW_], k_raw[:, o:o + HW_],
                                         kcw_sb[:, 0:1], kcb_sb[:], ALU.mult, ALU.add)
                        ve.scalar_tensor_tensor(acc_b[0:64, 0:HW_], k_raw[:, o + 1:o + 1 + HW_],
                                                kcw_sb[:, 1:2], acc_a[0:64, 0:HW_], ALU.mult, ALU.add)
                        ve.scalar_tensor_tensor(k_c[:, o:o + HW_], k_raw[:, o + 2:o + 2 + HW_],
                                                kcw_sb[:, 2:3], acc_b[0:64, 0:HW_], ALU.mult, ALU.add)
                        ve.tensor_scalar(acc_a[64:128, 0:HW_], v_raw[64:128, o:o + HW_],
                                         vcw_sb[64:128, 0:1], vcb_sb[64:128, :], ALU.mult, ALU.add)
                        ve.scalar_tensor_tensor(acc_b[64:128, 0:HW_], v_raw[64:128, o + 1:o + 1 + HW_],
                                                vcw_sb[64:128, 1:2], acc_a[64:128, 0:HW_], ALU.mult, ALU.add)
                        ve.scalar_tensor_tensor(v_c[64:128, o:o + HW_],
                                                v_raw[64:128, o + 2:o + 2 + HW_],
                                                vcw_sb[64:128, 2:3], acc_b[64:128, 0:HW_], ALU.mult, ALU.add)
                    kd_tok = spd.issue(sp, lambda: sp.dma_start(out=k_dup[64:128, :], in_=k_c[:, :]))
                    ve.tensor_copy(k_dup[0:64, :], k_c[:, :])
                    ve.memset(v_tok[:, :, 64:65], 1.0)
                    for sc8 in range(0, TPS, 8):
                        for j in range(8):
                            sc = sc8 + j
                            pe.transpose(ps_tr[:, j, 0:64],
                                         v_c[64:128, sc * 128:(sc + 1) * 128],
                                         const_bf[64:128, 64:128])
                        ve.tensor_copy(v_tok[:, sc8:sc8 + 8, 0:64], ps_tr[:, 0:8, 0:64])
                    DmaChan.wait_tok(nc.tensor, kd_tok)
                    DmaChan.wait_tok(nc.vector, kd_tok)

                if B_PHASE < 2:
                    continue
                with contextlib.ExitStack() as pa:
                    ps_st = pa.enter_context(nc.psum_tensor(f"ps_stB{s_}", [128, 4, 512], F32))
                    ps_pv = pa.enter_context(nc.psum_tensor(f"ps_pvB{s_}", [128, 8, 128], F32))
                    ps_tr2 = pa.enter_context(nc.psum_tensor(f"ps_tr2B{s_}", [128, 8, 128], BF16))
                    ps_o = ps_st
                    for cc_ in range(min(TPS, B_QC)):
                        n = s_ * TPS + cc_
                        rtok = spd.issue(sp, lambda n=n: sp.dma_start(
                            out=xt2[:, n % 2, :], in_=xres[n * 128:(n + 1) * 128, :]))
                        for sc in range(cc_ + 1):
                            for gi in range(4):
                                par_odd = gi >= 2
                                fc0 = (gi % 2) * 4
                                if not par_odd:
                                    lhs = k_c[:, sc * 128:(sc + 1) * 128]
                                    rhs = q_c[0:64, fc0:fc0 + 4, cc_ * 128:(cc_ + 1) * 128]
                                else:
                                    lhs = k_dup[64:128, sc * 128:(sc + 1) * 128]
                                    rhs = q_c[64:128, fc0:fc0 + 4, cc_ * 128:(cc_ + 1) * 128]
                                ch["pe"].inc(pe.matmul(ps_st[:, gi, :], lhs, rhs,
                                                       start=True, stop=True))
                            se.activation(pt[:].rearrange("p a b -> p (a b)"),
                                          ps_st[:].rearrange("p a b -> p (a b)"),
                                          AF.Exp, scale=0.125)
                            if sc == cc_:
                                tri_rep = trim_sb[:, :].unsqueeze(1).broadcast_to([128, 16, 128])
                                ve.tensor_tensor(pt[:], pt[:], tri_rep, ALU.mult)
                            for rnd in range(2):
                                for hj in range(8):
                                    hi = rnd * 8 + hj
                                    ch["pe"].inc(pe.matmul(ps_pv[:, hj, 0:65],
                                                           pt[:, hi, :], v_tok[:, sc, 0:65],
                                                           start=True, stop=True))
                                dstv = y_acc[:, rnd * 8:(rnd + 1) * 8, 0:65]
                                srcv = ps_pv[:, :, 0:65]
                                if sc == 0:
                                    ve.tensor_copy(dstv, srcv)
                                else:
                                    ve.tensor_tensor(dstv, dstv, srcv, ALU.add)
                        ve.reciprocal(rec[:, :].rearrange("p (a b) -> p a b", b=1), y_acc[:, :, 64:65])
                        rec_rep = rec[:, :].unsqueeze(2).broadcast_to([128, 16, 64])
                        ve.tensor_tensor(y_acc[:, :, 0:64], y_acc[:, :, 0:64], rec_rep, ALU.mult)
                        yv = y_tok[:].rearrange("p (a two) c -> p a two c", two=2)
                        ve.tensor_copy(yv[:, :, 0, :], y_acc[:, 0:8, 0:64])
                        ve.tensor_copy(yv[:, :, 1, :], y_acc[:, 8:16, 0:64])
                        for k in range(8):
                            pe.transpose(ps_tr2[:, k, :],
                                         y_tok[:, :, :].rearrange("p a b -> p (a b)")[:, k * 128:(k + 1) * 128],
                                         const_bf[:])
                        ve.tensor_copy(y_fmT[:], ps_tr2[:])
                        for half in range(2):
                            for k in range(8):
                                ch["pe"].inc(pe.matmul(ps_o[:, half, :], y_fmT[:, k, :],
                                                       wcp_sb[:, k, half * 512:(half + 1) * 512],
                                                       start=(k == 0), stop=(k == 7)))
                        if xstore_tok is not None:
                            DmaChan.wait_tok(nc.vector, xstore_tok)
                        DmaChan.wait_tok(nc.vector, rtok)
                        ve.tensor_tensor(xnew[:, 0:512], ps_o[:, 0, :], xt2[:, n % 2, 0:512], ALU.add)
                        ve.tensor_tensor(xnew[:, 512:1024], ps_o[:, 1, :], xt2[:, n % 2, 512:1024], ALU.add)
                        xstore_tok = spd.issue(sp, lambda n=n: sp.dma_start(
                            out=xres[n * 128:(n + 1) * 128, :], in_=xnew[:]))
            spd.wait_all(sp)
          stage_boundary()

        if dump == 0 or dump >= 3:
          with contextlib.ExitStack() as sc:
            t = lambda name, shape, dtype=F32: sc.enter_context(nc.sbuf_tensor(name, shape, dtype))
            wkey_sb = t("wkey_sb", [128, 8, FFN], BF16)
            wval_sb = t("wval_sb", [128, 32, D], BF16)
            wrec_sb = t("wrec_sb", [128, 8, D], BF16)
            maak_sb = t("maak_sb", [128, 8])
            maar_sb = t("maar_sb", [128, 8])
            xt2 = t("xt2C", [128, 2, D])
            x0t = t("x0tC", [128, 2, D])
            sq = t("sqC", [128, D])
            stats = t("statsC", [128, 4])
            hb = t("hbC", [128, D], BF16)
            h_fm = t("h_fmC", [128, 8, 128], BF16)
            hsh = t("hshC", [128, 8, 128], BF16)
            tailc = t("tailC", [128, 8, 1], BF16)
            xx = t("xxC", [128, 8, 128])
            xk_fm = t("xk_fmC", [128, 8, 128], BF16)
            xr_fm = t("xr_fmC", [128, 8, 128], BF16)
            kf = t("kfC", [128, 32, 128], BF16)
            tmqf = t("tmqfC", [128, 512])
            tmq = t("tmqC", [128, 512], BF16)
            sig = t("sigC", [128, D])
            t1 = t("t1C", [128, D])
            xnew = t("xnewC", [128, D])

            ps_kN = sc.enter_context(nc.psum_tensor("ps_kC", [128, 2, 512], F32))
            ps_kv = sc.enter_context(nc.psum_tensor("ps_kvC", [128, 2, 512], F32))
            ps_rec = sc.enter_context(nc.psum_tensor("ps_recC", [128, 2, 512], F32))
            ps_tr = sc.enter_context(nc.psum_tensor("ps_trC", [128, 8, 128], BF16))

            for k in range(8):
                gpd.issue(gp, lambda k=k: gp.dma_start(out=wkey_sb[:, k, :],
                                                       in_=w_key[k * 128:(k + 1) * 128, :]))
                gpd.issue(gp, lambda k=k: gp.dma_start(out=wrec_sb[:, k, :],
                                                       in_=w_rec[k * 128:(k + 1) * 128, :]))
            for k in range(32):
                gpd.issue(gp, lambda k=k: gp.dma_start(out=wval_sb[:, k, :],
                                                       in_=w_val[k * 128:(k + 1) * 128, :]))
            spd.issue(sp, lambda: sp.dma_start(out=maak_sb[:], in_=maa_k[:, :]))
            spd.issue(sp, lambda: sp.dma_start(out=maar_sb[:], in_=maa_r[:, :]))
            spd.wait_all(nc.vector)
            gpd.wait_all(nc.tensor)
            gpd.wait_all(nc.vector)

            xstore_tok = None
            for n in range(NT):
                s_, cc_ = divmod(n, TPS)
                rtok = spd.issue(sp, lambda n=n: sp.dma_start(
                    out=xt2[:, n % 2, :], in_=xres[n * 128:(n + 1) * 128, :]))
                rtok0 = spd.issue(sp, lambda n=n: sp.dma_start(
                    out=x0t[:, n % 2, :], in_=x_in[n * 128:(n + 1) * 128, :]))
                DmaChan.wait_tok(nc.vector, rtok)
                DmaChan.wait_tok(nc.scalar, rtok)
                xtile = xt2[:, n % 2, :]
                se.activation(sq[:], xtile, AF.Square, accum_out=stats[:, 0:1])
                ve.tensor_scalar(stats[:, 0:1], stats[:, 0:1], 1.0 / D, RMS_EPS, ALU.mult, ALU.add)
                se.activation(stats[:, 0:1], stats[:, 0:1], AF.Sqrt)
                ve.reciprocal(stats[:, 0:1], stats[:, 0:1])
                ve.tensor_scalar(hb[:], xtile, stats[:, 0:1], None, ALU.mult)
                for k in range(8):
                    pe.transpose(ps_tr[:, k, :], hb[:, k * 128:(k + 1) * 128], const_bf[:])
                ve.tensor_copy(h_fm[:], ps_tr[:])
                if cc_ == 0:
                    ve.memset(tailc[:], 0.0)
                ve.tensor_tensor(xx[:, :, 0:1], tailc[:], h_fm[:, :, 0:1], ALU.subtract)
                ve.tensor_tensor(xx[:, :, 1:128], h_fm[:, :, 0:127], h_fm[:, :, 1:128],
                                 ALU.subtract)
                ve.tensor_copy(tailc[:], h_fm[:, :, 127:128])
                maak_b = maak_sb[:].unsqueeze(2).broadcast_to([128, 8, 128])
                maar_b = maar_sb[:].unsqueeze(2).broadcast_to([128, 8, 128])
                ve.tensor_tensor(xk_fm[:], xx[:], maak_b, ALU.mult)
                ve.tensor_tensor(xk_fm[:], xk_fm[:], h_fm[:], ALU.add)
                ve.tensor_tensor(xr_fm[:], xx[:], maar_b, ALU.mult)
                ve.tensor_tensor(xr_fm[:], xr_fm[:], h_fm[:], ALU.add)
                for fg in range(8):
                    dst = ps_kN[:, fg % 2, :]
                    for k in range(8):
                        pe.matmul(dst, xk_fm[:, k, :],
                                  wkey_sb[:, k, fg * 512:(fg + 1) * 512],
                                  start=(k == 0), stop=(k == 7))
                    ve.tensor_scalar(tmqf[:], dst, 0.0, None, ALU.max)
                    ve.tensor_tensor(tmq[:], tmqf[:], tmqf[:], ALU.mult)
                    base = (fg % 2) * 4
                    for j in range(4):
                        pe.transpose(ps_tr[:, base + j, :],
                                     tmq[:, j * 128:(j + 1) * 128], const_bf[:])
                    ve.tensor_copy(kf[:, fg * 4:(fg + 1) * 4, :], ps_tr[:, base:base + 4, :])
                for half in range(2):
                    for kc in range(32):
                        pe.matmul(ps_kv[:, half, :], kf[:, kc, :],
                                  wval_sb[:, kc, half * 512:(half + 1) * 512],
                                  start=(kc == 0), stop=(kc == 31))
                for half in range(2):
                    for k in range(8):
                        pe.matmul(ps_rec[:, half, :], xr_fm[:, k, :],
                                  wrec_sb[:, k, half * 512:(half + 1) * 512],
                                  start=(k == 0), stop=(k == 7))
                se.activation(sig[:], ps_rec[:].rearrange("p a b -> p (a b)"), AF.Sigmoid)
                ve.tensor_tensor(t1[:], sig[:], ps_kv[:].rearrange("p a b -> p (a b)"), ALU.mult)
                if xstore_tok is not None:
                    DmaChan.wait_tok(nc.vector, xstore_tok)
                ve.tensor_tensor(xnew[:], t1[:], xtile, ALU.add)
                DmaChan.wait_tok(nc.vector, rtok0)
                ve.tensor_tensor(xnew[:], xnew[:], x0t[:, n % 2, :], ALU.add)
                xstore_tok = spd.issue(sp, lambda n=n: sp.dma_start(
                    out=xres[n * 128:(n + 1) * 128, :], in_=xnew[:]))
            spd.wait_all(sp)
          stage_boundary()

        if dump == 0 or dump >= 4:
          rg = [list(range(NCORES))]
          with contextlib.ExitStack() as sd_:
            t = lambda name, shape, dtype=F32: sd_.enter_context(nc.sbuf_tensor(name, shape, dtype))
            rw_sb = t("rw_sbD", [128, 8, E])
            ones_f = t("ones_fD", [128, 1])
            xt = t("xtD", [128, D])
            h1 = t("h1D", [128, D])
            sq = t("sqD", [128, D])
            stats = t("statsD", [128, 8])
            h1b = t("h1bD", [128, D], BF16)
            h1T = t("h1TD", [128, 8, 128])
            lg = t("lgD", [128, E])
            prb = t("prbD", [128, E])
            exl = t("exlD", [128, E])
            eq = t("eqD", [128, E])
            tt_ = t("ttD", [128, E])
            oh0 = t("oh0D", [128, E])
            oh1 = t("oh1D", [128, E])
            ohb = t("ohbD", [128, E])
            idxw_sb = t("idxwD", [128, NT, 4])
            ctile_sb = t("ctileD", [128, NT, E])
            tilesum_sb = t("tilesumD", [32, E])
            cnt_ag = t("cntagD", [NCORES, E])
            tb_sb = t("tbD", [32, E])
            tbrow = t("tbrowD", [1, E])
            cumf = t("cumfD", [128, E])
            rk = t("rkD", [128, 8])
            dest_u = t("destuD", [128, 2], U32)
            slots_sb = t("slotsD", [128, NT, 2])
            wgts_sb = t("wgtsD", [128, NT, 2])
            zz = t("zzD", [128, E * CAP // 128])
            sel32_sb = t("sel32D", [128, NT * 32])
            sel1_sb = t("sel1D", [32, NT * 128])

            routing_ps = contextlib.ExitStack()
            ps_r = routing_ps.enter_context(nc.psum_tensor("ps_rD", [128, 512], F32))
            ps_trD = routing_ps.enter_context(nc.psum_tensor("ps_trD", [128, 8, 128], F32))
            ps_ts = routing_ps.enter_context(nc.psum_tensor("ps_tsD", [32, E], F32))

            ve.memset(ones_f[:], 1.0)
            breg = gp.to_reg(E * CAP - 1)
            spd.issue(sp, lambda: sp.dma_start(out=sel32_sb[:], in_=sel32[:, :]))
            spd.issue(sp, lambda: sp.dma_start(out=sel1_sb[:], in_=sel1col[:, :]))
            for k in range(8):
                spd.issue(sp, lambda k=k: sp.dma_start(out=rw_sb[:, k, :],
                                                       in_=router_w[k * 128:(k + 1) * 128, :]))
            spd.wait_all(nc.tensor)
            spd.wait_all(nc.vector)
            ve.memset(zz[:], 0.0)
            ztok = gpd.issue(gp, lambda: gp.dma_start(
                out=slot_in.rearrange("(p n) o -> p (n o)", p=128), in_=zz[:]))
            zyb = t("zybD", [128, D], BF16)
            ve.memset(zyb[:], 0.0)
            for r0 in range(0, 2 * CAP, 128):
                gpd.issue(gp, lambda r0=r0: gp.dma_start(
                    out=agy_in[r0:r0 + 128, :], in_=zyb[:]))

            h1b_tok = None
            for n in range(NT):
                ltok = spd.issue(sp, lambda n=n: sp.dma_start(
                    out=xt[:], in_=xres[n * 128:(n + 1) * 128, :]))
                DmaChan.wait_tok(nc.vector, ltok)
                DmaChan.wait_tok(nc.scalar, ltok)
                ln_tile(xt, h1, sq, stats, LN_EPS, True)
                if h1b_tok is not None:
                    DmaChan.wait_tok(nc.vector, h1b_tok)
                ve.tensor_copy(h1b[:], h1[:])
                h1b_tok = spd.issue(sp, lambda n=n: sp.dma_start(
                    out=agh_in[n * 128:(n + 1) * 128, :], in_=h1b[:]))
                for k in range(8):
                    pe.transpose(ps_trD[:, k, :], h1[:, k * 128:(k + 1) * 128], identf_sb[:])
                for k in range(8):
                    ve.tensor_copy(h1T[:, k, :], ps_trD[:, k, :])
                for k in range(8):
                    pe.matmul(ps_r[:, 0:E], h1T[:, k, :], rw_sb[:, k, :],
                              start=(k == 0), stop=(k == 7))
                ve.tensor_tensor(lg[:], ps_r[:, 0:E], rb_sb[:], ALU.add)
                ve.reduce_max(stats[:, 0:1], lg[:], axis=AX.X)
                ve.tensor_scalar(stats[:, 0:1], stats[:, 0:1], -1.0, None, ALU.mult)
                se.activation(exl[:], lg[:], AF.Exp, bias=stats[:, 0:1],
                              accum_out=stats[:, 1:2])
                ve.reciprocal(stats[:, 1:2], stats[:, 1:2])
                ve.tensor_scalar(prb[:], exl[:], stats[:, 1:2], None, ALU.mult)
                ve.reduce_max(stats[:, 2:3], prb[:], axis=AX.X)
                ve.tensor_scalar(eq[:], prb[:], stats[:, 2:3], None, ALU.is_ge)
                ve.tensor_scalar(tt_[:], eq[:], -1e9, 1e9, ALU.mult, ALU.add)
                ve.tensor_tensor(tt_[:], tt_[:], iota16_sb[:], ALU.add)
                ve.tensor_reduce(idxw_sb[:, n, 0:1], tt_[:], axis=AX.X, op=ALU.min)
                ve.tensor_scalar(oh0[:], iota16_sb[:], idxw_sb[:, n, 0:1], None, ALU.is_equal)
                ve.tensor_tensor(tt_[:], prb[:], oh0[:], ALU.mult)
                ve.tensor_reduce(stats[:, 3:4], tt_[:], axis=AX.X, op=ALU.add)
                ve.scalar_tensor_tensor(tt_[:], oh0[:], -1e9, prb[:], ALU.mult, ALU.add)
                ve.reduce_max(stats[:, 4:5], tt_[:], axis=AX.X)
                ve.tensor_scalar(eq[:], tt_[:], stats[:, 4:5], None, ALU.is_ge)
                ve.tensor_scalar(eq[:], eq[:], -1e9, 1e9, ALU.mult, ALU.add)
                ve.tensor_tensor(eq[:], eq[:], iota16_sb[:], ALU.add)
                ve.tensor_reduce(idxw_sb[:, n, 1:2], eq[:], axis=AX.X, op=ALU.min)
                ve.tensor_scalar(oh1[:], iota16_sb[:], idxw_sb[:, n, 1:2], None, ALU.is_equal)
                ve.tensor_tensor(tt_[:], prb[:], oh1[:], ALU.mult)
                ve.tensor_reduce(stats[:, 5:6], tt_[:], axis=AX.X, op=ALU.add)
                ve.tensor_tensor(stats[:, 6:7], stats[:, 3:4], stats[:, 5:6], ALU.add)
                ve.tensor_scalar(stats[:, 6:7], stats[:, 6:7], 1e-9, None, ALU.add)
                ve.reciprocal(stats[:, 6:7], stats[:, 6:7])
                ve.tensor_tensor(idxw_sb[:, n, 2:3], stats[:, 3:4], stats[:, 6:7], ALU.mult)
                ve.tensor_tensor(idxw_sb[:, n, 3:4], stats[:, 5:6], stats[:, 6:7], ALU.mult)
                ve.tensor_tensor(ohb[:], oh0[:], oh1[:], ALU.add)
                pe.matmul(ps_r[:, 16:32], ltri_sb[:], ohb[:], start=True, stop=True)
                ve.tensor_copy(ctile_sb[:, n, :], ps_r[:, 16:32])
                pe.matmul(ps_ts[:, :], sel32_sb[:, n * 32:(n + 1) * 32], ohb[:],
                          start=(n == 0), stop=(n == NT - 1))

            ve.tensor_copy(tilesum_sb[:], ps_ts[:, :])
            pe.matmul(ps_r[0:1, 48:64], ones_f[0:32, :], tilesum_sb[:], start=True, stop=True)
            ve.tensor_copy(tbrow[:], ps_r[0:1, 48:64])
            ctok = gpd.issue(gp, lambda: gp.dma_start(out=cnt_in[:, :], in_=tbrow[:]))
            DmaChan.wait_tok(gp, ctok)
            ch["cc"].inc(gp.collective_compute(
                "AllGather", ALU.bypass, replica_groups=rg,
                ins=[cnt_in.opt()], outs=[cnt_out.opt()]))
            cc_cnt = ch["cc"].n
            spd.wait_all(gp)
            ch["cc"].inc(gp.collective_compute(
                "AllGather", ALU.bypass, replica_groups=rg,
                ins=[agh_in.opt()], outs=[agh_out.opt()]))
            cc_h = ch["cc"].n
            ch["cc"].wait(sp, cc_cnt)
            cgtok = spd.issue(sp, lambda: sp.dma_start(out=cnt_ag[:], in_=cnt_out[:, :]))
            DmaChan.wait_tok(nc.tensor, cgtok)
            pe.matmul(ps_r[0:32, 64:80], ltri32_sb[:], tilesum_sb[:], start=True, stop=False)
            pe.matmul(ps_r[0:32, 64:80], pm32_sb[:], cnt_ag[:], start=False, stop=True)
            ve.tensor_copy(tb_sb[:], ps_r[0:32, 64:80])

            sc_toks = []
            for n in range(NT):
                pe.matmul(ps_r[:, 80:96], sel1_sb[:, n * 128:(n + 1) * 128], tb_sb[:],
                          start=True, stop=True)
                ve.tensor_tensor(cumf[:], ctile_sb[:, n, :], ps_r[:, 80:96], ALU.add)
                for tkk in sc_toks:
                    DmaChan.wait_tok(nc.vector, tkk)
                sc_toks = []
                for kk in range(2):
                    ve.tensor_scalar(oh0[:], iota16_sb[:], idxw_sb[:, n, kk:kk + 1],
                                     None, ALU.is_equal)
                    ve.tensor_tensor(tt_[:], cumf[:], oh0[:], ALU.mult)
                    ve.tensor_reduce(rk[:, kk:kk + 1], tt_[:], axis=AX.X, op=ALU.add)
                    ve.tensor_scalar(rk[:, 2 + kk:3 + kk], rk[:, kk:kk + 1], float(CAP),
                                     None, ALU.is_lt)
                    ve.tensor_tensor(wgts_sb[:, n, kk:kk + 1], idxw_sb[:, n, 2 + kk:3 + kk],
                                     rk[:, 2 + kk:3 + kk], ALU.mult)
                    ve.tensor_scalar(rk[:, 4 + kk:5 + kk], rk[:, kk:kk + 1],
                                     float(CAP - 1), None, ALU.min)
                    ve.tensor_scalar(slots_sb[:, n, kk:kk + 1], idxw_sb[:, n, kk:kk + 1],
                                     float(CAP), rk[:, 4 + kk:5 + kk], ALU.mult, ALU.add)
                    ve.tensor_scalar(rk[:, 6 + kk:7 + kk], rk[:, 2 + kk:3 + kk],
                                     -1e6, 1e6, ALU.mult, ALU.add)
                    ve.tensor_scalar(rk[:, 4 + kk:5 + kk], idxw_sb[:, n, kk:kk + 1],
                                     float(CAP), rk[:, kk:kk + 1], ALU.mult, ALU.add)
                    ve.tensor_tensor(rk[:, 4 + kk:5 + kk], rk[:, 4 + kk:5 + kk],
                                     rk[:, 6 + kk:7 + kk], ALU.add)
                    ve.tensor_copy(dest_u[:, kk:kk + 1], rk[:, 4 + kk:5 + kk])
                DmaChan.wait_tok(gp, ztok)
                for kk in range(2):
                    sc_toks.append(gpd.issue(gp, lambda n=n, kk=kk: gp.indirect_dma_start(
                        out=slot_in,
                        out_offset=bass.IndirectOffsetOnAxis(ap=dest_u[:, kk:kk + 1], axis=0),
                        in_=tid_sb[:, n:n + 1], in_offset=None,
                        bounds_check=breg, oob_is_err=False)))

            routing_ps.close()
            gpd.wait_all(gp)
            ch["cc"].inc(gp.collective_compute(
                "ReduceScatter", ALU.add, replica_groups=rg,
                ins=[slot_in.opt()], outs=[rs_out.opt()]))
            cc_rs = ch["cc"].n

            with contextlib.ExitStack() as se_:
                t2 = lambda name, shape, dtype=F32: se_.enter_context(nc.sbuf_tensor(name, shape, dtype))
                wg_sb = t2("wg_sbD", [128, 8, FE], BF16)
                wu_sb = t2("wu_sbD", [128, 8, FE], BF16)
                wd_sb = t2("wd_sbD", [128, 11, D], BF16)
                tid_e = t2("tid_eD", [128, CAP // 128])
                tidu = t2("tiduD", [128, CAP // 128], U32)
                hrow = t2("hrowD", [128, 4, D], BF16)
                hfm_e = t2("hfmD", [128, 8, 512], BF16)
                sg = t2("sgD", [128, 512], BF16)
                gu_sb = t2("guD", [128, 11, 512], BF16)
                y_e = t2("y_eD", [128, 2, D], BF16)
                ps_g = se_.enter_context(nc.psum_tensor("ps_gD", [128, 512], F32))
                ps_u = se_.enter_context(nc.psum_tensor("ps_uD", [128, 512], F32))
                ps_d = se_.enter_context(nc.psum_tensor("ps_dD", [128, 2, 1024], F32))
                ps_t2 = se_.enter_context(nc.psum_tensor("ps_t2D", [128, 4, 128], BF16))

                NGRP = CAP // 512

                def load_shared_w():
                    for k in range(8):
                        gpd.issue(gp, lambda k=k: gp.dma_start(
                            out=wg_sb[:, k, :], in_=sw_gate[k * 128:(k + 1) * 128, :]))
                        gpd.issue(gp, lambda k=k: gp.dma_start(
                            out=wu_sb[:, k, :], in_=sw_up[k * 128:(k + 1) * 128, :]))
                    for k in range(11):
                        gpd.issue(gp, lambda k=k: gp.dma_start(
                            out=wd_sb[:, k, :], in_=sw_down[k * 128:(k + 1) * 128, :]))
                    gpd.wait_all(nc.tensor)
                    gpd.wait_all(nc.vector)

                sstore = []

                def shared_grp(grp):
                    htoks = []
                    for i in range(4):
                        r0 = grp * 512 + i * 128
                        htoks.append(spd.issue(sp, lambda r0=r0, i=i: sp.dma_start(
                            out=hrow[:, i, :], in_=agh_in[r0:r0 + 128, :])))
                    for ht in htoks:
                        DmaChan.wait_tok(nc.tensor, ht)
                    for i in range(4):
                        for k in range(8):
                            pe.transpose(ps_t2[:, k % 4, :], hrow[:, i, k * 128:(k + 1) * 128],
                                         const_bf[:])
                            ve.tensor_copy(hfm_e[:, k, i * 128:(i + 1) * 128],
                                           ps_t2[:, k % 4, :])
                    for fc in range(11):
                        for k in range(8):
                            pe.matmul(ps_g[:, :], wg_sb[:, k, fc * 128:(fc + 1) * 128],
                                      hfm_e[:, k, :], start=(k == 0), stop=(k == 7))
                            pe.matmul(ps_u[:, :], wu_sb[:, k, fc * 128:(fc + 1) * 128],
                                      hfm_e[:, k, :], start=(k == 0), stop=(k == 7))
                        se.activation(sg[:], ps_g[:, :], AF.Silu)
                        ve.tensor_tensor(gu_sb[:, fc, :], sg[:], ps_u[:, :], ALU.mult)
                    for i in range(4):
                        pd = ps_d[:, i % 2, :]
                        for kc in range(11):
                            for half in range(2):
                                pe.matmul(pd[:, half * 512:(half + 1) * 512],
                                          gu_sb[:, kc, i * 128:(i + 1) * 128],
                                          wd_sb[:, kc, half * 512:(half + 1) * 512],
                                          start=(kc == 0), stop=(kc == 10))
                        if len(sstore) >= 2:
                            DmaChan.wait_tok(nc.vector, sstore[-2])
                        ve.tensor_copy(y_e[:, i % 2, :], pd[:, :])
                        r0 = grp * 512 + i * 128
                        sstore.append(spd.issue(sp, lambda r0=r0, i=i: sp.dma_start(
                            out=shared_dram[r0:r0 + 128, :], in_=y_e[:, i % 2, :])))

                spd.wait_all(sp)
                load_shared_w()
                for grp in range(4):
                    shared_grp(grp)

                ch["cc"].wait(sp, cc_rs)
                ch["cc"].wait(gp, cc_h)
                ystore = []
                for ee in range(2):
                    for k in range(8):
                        gpd.issue(gp, lambda ee=ee, k=k: gp.dma_start(
                            out=wg_sb[:, k, :], in_=my_wg[ee, k * 128:(k + 1) * 128, :]))
                        gpd.issue(gp, lambda ee=ee, k=k: gp.dma_start(
                            out=wu_sb[:, k, :], in_=my_wu[ee, k * 128:(k + 1) * 128, :]))
                    for k in range(11):
                        gpd.issue(gp, lambda ee=ee, k=k: gp.dma_start(
                            out=wd_sb[:, k, :], in_=my_wd[ee, k * 128:(k + 1) * 128, :]))
                    ttok = spd.issue(sp, lambda ee=ee: sp.dma_start(
                        out=tid_e[:],
                        in_=rs_out[ee * CAP:(ee + 1) * CAP, :]
                            .rearrange("(p n) o -> p (n o)", p=128)))
                    DmaChan.wait_tok(nc.vector, ttok)
                    ve.tensor_copy(tidu[:], tid_e[:])
                    gpd.wait_all(nc.tensor)
                    gpd.wait_all(nc.vector)
                    for grp in range(NGRP):
                        gtoks = []
                        for i in range(4):
                            j = grp * 4 + i
                            gtoks.append(gpd.issue(gp, lambda i=i, j=j: gp.indirect_dma_start(
                                out=hrow[:, i, :], out_offset=None,
                                in_=agh_out,
                                in_offset=bass.IndirectOffsetOnAxis(ap=tidu[:, j:j + 1], axis=0))))
                        for gt in gtoks:
                            DmaChan.wait_tok(nc.tensor, gt)
                        for i in range(4):
                            for k in range(8):
                                pe.transpose(ps_t2[:, k % 4, :], hrow[:, i, k * 128:(k + 1) * 128],
                                             const_bf[:])
                                ve.tensor_copy(hfm_e[:, k, i * 128:(i + 1) * 128],
                                               ps_t2[:, k % 4, :])
                        for fc in range(11):
                            for k in range(8):
                                pe.matmul(ps_g[:, :], wg_sb[:, k, fc * 128:(fc + 1) * 128],
                                          hfm_e[:, k, :], start=(k == 0), stop=(k == 7))
                                pe.matmul(ps_u[:, :], wu_sb[:, k, fc * 128:(fc + 1) * 128],
                                          hfm_e[:, k, :], start=(k == 0), stop=(k == 7))
                            se.activation(sg[:], ps_g[:, :], AF.Silu)
                            ve.tensor_tensor(gu_sb[:, fc, :], sg[:], ps_u[:, :], ALU.mult)
                        for i in range(4):
                            pd = ps_d[:, i % 2, :]
                            for kc in range(11):
                                for half in range(2):
                                    pe.matmul(pd[:, half * 512:(half + 1) * 512],
                                              gu_sb[:, kc, i * 128:(i + 1) * 128],
                                              wd_sb[:, kc, half * 512:(half + 1) * 512],
                                              start=(kc == 0), stop=(kc == 10))
                            if len(ystore) >= 2:
                                DmaChan.wait_tok(nc.vector, ystore[-2])
                            ve.tensor_copy(y_e[:, i % 2, :], pd[:, :])
                            j = grp * 4 + i
                            row0 = ee * CAP + j
                            ystore.append(spd.issue(sp, lambda row0=row0, i=i: sp.dma_start(
                                out=agy_in[row0:row0 + (128 - 1) * (CAP // 128) + 1:CAP // 128, :],
                                in_=y_e[:, i % 2, :])))
                spd.wait_all(gp)
                gpd.wait_all(gp)
                ch["cc"].inc(gp.collective_compute(
                    "AllGather", ALU.bypass, replica_groups=rg,
                    ins=[agy_in.opt()], outs=[agy_out.opt()]))
                cc_y = ch["cc"].n

                load_shared_w()
                for grp in range(4, NTOK // 512):
                    shared_grp(grp)

            with contextlib.ExitStack() as sf_:
                t3 = lambda name, shape, dtype=F32: sf_.enter_context(nc.sbuf_tensor(name, shape, dtype))
                sh_b = t3("sh_bD", [128, D], BF16)
                y0t = t3("y0tD", [128, D], BF16)
                y1t = t3("y1tD", [128, D], BF16)
                su = t3("suD", [128, 2], U32)
                acc = t3("accD", [128, D])
                otok = None
                ch["cc"].wait(gp, cc_y)
                for n in range(NT):
                    ltok = spd.issue(sp, lambda n=n: sp.dma_start(
                        out=xt[:], in_=xres[n * 128:(n + 1) * 128, :]))
                    stok_ = spd.issue(sp, lambda n=n: sp.dma_start(
                        out=sh_b[:], in_=shared_dram[n * 128:(n + 1) * 128, :]))
                    if otok is not None:
                        DmaChan.wait_tok(nc.vector, otok)
                    ve.tensor_copy(su[:], slots_sb[:, n, :])
                    g0 = gpd.issue(gp, lambda: gp.indirect_dma_start(
                        out=y0t[:], out_offset=None, in_=agy_out,
                        in_offset=bass.IndirectOffsetOnAxis(ap=su[:, 0:1], axis=0)))
                    g1 = gpd.issue(gp, lambda: gp.indirect_dma_start(
                        out=y1t[:], out_offset=None, in_=agy_out,
                        in_offset=bass.IndirectOffsetOnAxis(ap=su[:, 1:2], axis=0)))
                    DmaChan.wait_tok(nc.vector, ltok)
                    DmaChan.wait_tok(nc.scalar, ltok)
                    ln_tile(xt, h1, sq, stats, LN_EPS, True)
                    DmaChan.wait_tok(nc.vector, g0)
                    ve.scalar_tensor_tensor(acc[:], y0t[:], wgts_sb[:, n, 0:1], xt[:],
                                            ALU.mult, ALU.add)
                    DmaChan.wait_tok(nc.vector, g1)
                    ve.scalar_tensor_tensor(acc[:], y1t[:], wgts_sb[:, n, 1:2], acc[:],
                                            ALU.mult, ALU.add)
                    ve.tensor_tensor(acc[:], acc[:], h1[:], ALU.add)
                    DmaChan.wait_tok(nc.vector, stok_)
                    ve.tensor_tensor(acc[:], acc[:], sh_b[:], ALU.add)
                    otok = sod.issue(sp, lambda n=n: sp.dma_start(
                        out=out_x[n * 128:(n + 1) * 128, :], in_=acc[:]))
                sod.wait_all(sp)
          stage_boundary()

        if dump in (1, 2, 3):
            with contextlib.ExitStack() as sd:
                tb = sd.enter_context(nc.sbuf_tensor("dumpbuf", [128, 2, D], F32))
                stok = [None, None]
                for n in range(NT):
                    b = n % 2
                    if stok[b] is not None:
                        DmaChan.wait_tok(sp, stok[b])
                    ltok = spd.issue(sp, lambda n=n, b=b: sp.dma_start(
                        out=tb[:, b, :], in_=xres[n * 128:(n + 1) * 128, :]))
                    DmaChan.wait_tok(sp, ltok)
                    stok[b] = sod.issue(sp, lambda n=n, b=b: sp.dma_start(
                        out=dmp[n * 128:(n + 1) * 128, :], in_=tb[:, b, :]))
                sod.wait_all(sp)
            stage_boundary()

        if dump:
            with contextlib.ExitStack() as so_:
                tb = so_.enter_context(nc.sbuf_tensor("outbuf", [128, 2, D], F32))
                stok = [None, None]
                for n in range(NT):
                    b = n % 2
                    if stok[b] is not None:
                        DmaChan.wait_tok(sp, stok[b])
                    ltok = spd.issue(sp, lambda n=n, b=b: sp.dma_start(
                        out=tb[:, b, :], in_=xres[n * 128:(n + 1) * 128, :]))
                    DmaChan.wait_tok(sp, ltok)
                    stok[b] = sod.issue(sp, lambda n=n, b=b: sp.dma_start(
                        out=out_x[n * 128:(n + 1) * 128, :], in_=tb[:, b, :]))
                sod.wait_all(sp)

    return nc


def make_consts(core_id):
    ident = np.eye(128, dtype=np.float32)
    ltri = np.triu(np.ones((128, 128), np.float32), 1)
    ltri32v = np.triu(np.ones((32, 32), np.float32), 1)
    nm = np.where(np.tril(np.ones((128, 128), np.float32)) > 0, 0.0, -1e30)
    nm = np.where(np.arange(128)[:, None] <= np.arange(128)[None, :], 0.0, -1e30).astype(np.float32)
    negmask_v = np.repeat(nm[:, None, :], 8, axis=1).copy()
    trim = (np.arange(128)[:, None] <= np.arange(128)[None, :]).astype(np.float32).astype(BFnp)
    iota16v = np.tile(np.arange(E, dtype=np.float32)[None, :], (128, 1))
    pm32 = np.zeros((NCORES, 32), np.float32)
    pm32[:core_id, :] = 1.0
    tid = (core_id * NTOK + (np.arange(NT)[None, :] * 128 + np.arange(128)[:, None])).astype(np.float32)
    sel32v = np.zeros((128, NT * 32), np.float32)
    for n in range(NT):
        sel32v[:, n * 32 + n] = 1.0
    sel1v = np.zeros((32, NT * 128), np.float32)
    for n in range(NT):
        sel1v[n, n * 128:(n + 1) * 128] = 1.0
    return {
        "sel32": sel32v,
        "sel1col": sel1v,
        "ident_bf": ident.astype(BFnp),
        "ident_f32": ident,
        "ltri_strict": ltri,
        "ltri32": ltri32v,
        "negmask": negmask_v,
        "trimask_bf": trim,
        "iota16": iota16v,
        "prevmask32": pm32,
        "tid_f32": tid,
        "ones_row": np.ones((1, 128), np.float32),
    }


def make_in_map(core_id, inputs):
    p = lambda k: np.ascontiguousarray(np.asarray(inputs[k], np.float32))
    m = {
        "x": np.ascontiguousarray(np.asarray(inputs["L0_attn_x"], np.float32)
                                  .reshape(-1, T, D)[BSEQ * core_id:BSEQ * (core_id + 1)]
                                  .reshape(NTOK, D)),
        "W_in": p("L0_attn_W_in"),
        "conv_w": p("L0_attn_conv_w"),
        "conv_b": p("L0_attn_conv_b").reshape(-1, 1),
        "A_log": p("L0_attn_A_log").reshape(-1, 1),
        "Dm_exp": np.ascontiguousarray(np.repeat(p("L0_attn_Dm"), PH).reshape(16, 128).T),
        "dt_bias": p("L0_attn_dt_bias").reshape(-1, 1),
        "mnorm_w": np.ascontiguousarray(p("L0_attn_mnorm_w").reshape(16, 128).T),
        "W_out": p("L0_attn_W_out"),
        "W_qkv": p("L0_attn_W_qkv"),
        "W_cproj": p("L0_attn_W_cproj"),
        "qconv_w": np.ascontiguousarray(np.tile(p("L0_attn_qconv_w"), (2, 1))),
        "qconv_b": np.ascontiguousarray(np.tile(p("L0_attn_qconv_b").reshape(-1, 1), (2, 1))),
        "kconv_w": p("L0_attn_kconv_w"),
        "kconv_b": p("L0_attn_kconv_b").reshape(-1, 1),
        "vconv_w": p("L0_attn_vconv_w"),
        "vconv_b": p("L0_attn_vconv_b").reshape(-1, 1),
        "time_maa_k": np.ascontiguousarray(p("L0_attn_time_maa_k").reshape(8, 128).T),
        "time_maa_r": np.ascontiguousarray(p("L0_attn_time_maa_r").reshape(8, 128).T),
        "W_key": p("L0_attn_W_key"),
        "W_rec": p("L0_attn_W_rec"),
        "W_val": p("L0_attn_W_val"),
        "router_w": p("L0_ffn_router_w"),
        "rb_bcast": np.ascontiguousarray(np.tile(p("L0_ffn_router_b").reshape(1, -1), (128, 1))),
        "my_wg": np.ascontiguousarray(p("L0_ffn_w_gate")[2 * core_id:2 * core_id + 2]),
        "my_wu": np.ascontiguousarray(p("L0_ffn_w_up")[2 * core_id:2 * core_id + 2]),
        "my_wd": np.ascontiguousarray(p("L0_ffn_w_down")[2 * core_id:2 * core_id + 2]),
        "sw_gate": p("L0_ffn_sw_gate"),
        "sw_up": p("L0_ffn_sw_up"),
        "sw_down": p("L0_ffn_sw_down"),
    }
    m.update(make_consts(core_id))
    return m


def kernel(**inputs):
    nc = build_nc()
    in_maps = [make_in_map(c, inputs) for c in range(NCORES)]
    res = run_bass_kernel_spmd(nc, in_maps, core_ids=list(range(NCORES)))
    out = np.concatenate([res.results[c]["out"].reshape(2, T, D) for c in range(NCORES)], axis=0)
    return out


# version 09 (final)
# speedup vs baseline: 1.0366x; 1.0366x over previous
import contextlib
import numpy as np
import ml_dtypes
import concourse.bass as bass
import concourse.mybir as mybir
from concourse.bass_utils import run_bass_kernel_spmd

F32 = mybir.dt.float32
BF16 = mybir.dt.bfloat16
U32 = mybir.dt.uint32
AF = mybir.ActivationFunctionType
ALU = mybir.AluOpType
AX = mybir.AxisListType
BFnp = ml_dtypes.bfloat16

B_PHASE = 2
B_QC = 99
SHARED_AG = True

D = 1024
T = 2048
BSEQ = 2
NTOK = BSEQ * T
NT = NTOK // 128
TPS = T // 128
DI = 2048
DS = 128
NH = 32
PH = 64
ZXB = 2 * DI + 2 * DS + NH
NHA = 16
HDA = 64
FFN = 4096
E = 16
FE = 1408
CAP = 5120
NTOT = 32768
NCORES = 8


def _recalc():
    global NTOT, CAP
    NTOT = NCORES * NTOK
    CAP = int(np.ceil(1.25 * NTOT * 2 / E))

LN_EPS = 1e-5
RMS_EPS = 1e-6
MN_EPS = 1e-5


class Ch:

    def __init__(self, stack, nc, name, step=1):
        self.sem = stack.enter_context(nc.semaphore(name))
        self.n = 0
        self.step = step

    def inc(self, instr):
        if isinstance(instr, int):
            return instr
        instr.then_inc(self.sem, self.step)
        self.n += self.step
        return self.n

    def wait(self, eng, n=None):
        v = self.n if n is None else n
        if v > 0:
            eng.wait_ge(self.sem, v)


_CHAINED_OPS = {
    "tensor_scalar", "tensor_tensor", "tensor_copy", "tensor_reduce", "reduce_max",
    "reduce_sum", "reciprocal", "memset", "memzero", "scalar_tensor_tensor",
    "tensor_tensor_scan", "select", "activation", "copy", "mul", "add", "sqrt",
    "square", "sign", "tensor_add", "tensor_sub", "tensor_mul", "tensor_max",
    "tensor_relu", "tensor_scalar_add", "tensor_scalar_mul", "tensor_scalar_max",
    "tensor_scalar_min", "tensor_scalar_sub", "tensor_single_scalar", "max",
    "max_with_indices", "iota", "bn_stats", "bn_aggr",
}


_PE_OPS = {"matmul", "transpose", "ldweights"}


class SelfChain:

    def __init__(self, eng, own, others, ops, self_wait=True):
        self._eng = eng
        self._own = own
        self._others = list(others)
        self._ops = ops
        self._self_wait = self_wait
        self._last = {}

    def _sync(self):
        chans = ([self._own] if (self._own is not None and self._self_wait) else []) + self._others
        for c in chans:
            if c.n > self._last.get(id(c), 0):
                self._eng.wait_ge(c.sem, c.n)
                self._last[id(c)] = c.n

    def reset(self):
        self._last = {}

    def __getattr__(self, name):
        f = getattr(self._eng, name)
        if name not in self._ops:
            return f

        def wrapped(*a, **kw):
            self._sync()
            instr = f(*a, **kw)
            if self._own is not None:
                self._own.inc(instr)
                return self._own.n
            return instr

        return wrapped


class DmaChan:

    def __init__(self, stack, nc, name, k=4, sync_chans=()):
        self.slots = [Ch(stack, nc, f"{name}{i}", 16) for i in range(k)]
        self.k = k
        self.i = 0
        self.sync_chans = list(sync_chans)
        self._last = {}

    def issue(self, eng, instr_fn):
        for c in self.sync_chans:
            if c.n > self._last.get(id(c), 0):
                eng.wait_ge(c.sem, c.n)
                self._last[id(c)] = c.n
        s = self.slots[self.i % self.k]
        self.i += 1
        if s.n > 0:
            eng.wait_ge(s.sem, s.n)
        s.inc(instr_fn())
        return (s, s.n)

    @staticmethod
    def wait_tok(eng, tok):
        s, n = tok
        eng.wait_ge(s.sem, n)

    def wait_all(self, eng):
        for s in self.slots:
            if s.n > 0:
                eng.wait_ge(s.sem, s.n)

    def reset(self):
        self.i = 0
        self._last = {}
        for s in self.slots:
            s.n = 0


def build_nc(dump=0):
    nc = bass.Bass("TRN2", target_bir_lowering=False)

    def inp(name, shape, dtype=F32):
        return nc.dram_tensor(name, shape, dtype, kind="ExternalInput").ap()

    x_in = inp("x", [NTOK, D])
    w_in = inp("W_in", [D, ZXB])
    conv_w = inp("conv_w", [DI + 2 * DS, 4])
    conv_b = inp("conv_b", [DI + 2 * DS, 1])
    a_log = inp("A_log", [NH, 1])
    dm_exp = inp("Dm_exp", [128, DI // 128])
    dt_bias = inp("dt_bias", [NH, 1])
    mnorm_w = inp("mnorm_w", [128, DI // 128])
    w_out = inp("W_out", [DI, D])
    w_qkv = inp("W_qkv", [D, D + 2 * HDA])
    w_cproj = inp("W_cproj", [D, D])
    qconv_w = inp("qconv_w", [128, 3])
    qconv_b = inp("qconv_b", [128, 1])
    kconv_w = inp("kconv_w", [HDA, 3])
    kconv_b = inp("kconv_b", [HDA, 1])
    vconv_w = inp("vconv_w", [HDA, 3])
    vconv_b = inp("vconv_b", [HDA, 1])
    maa_k = inp("time_maa_k", [128, D // 128])
    maa_r = inp("time_maa_r", [128, D // 128])
    w_key = inp("W_key", [D, FFN])
    w_rec = inp("W_rec", [D, D])
    w_val = inp("W_val", [FFN, D])
    router_w = inp("router_w", [D, E])
    rb_bcast = inp("rb_bcast", [128, E])
    my_wg = inp("my_wg", [2, D, FE])
    my_wu = inp("my_wu", [2, D, FE])
    my_wd = inp("my_wd", [2, FE, D])
    sw_gate = inp("sw_gate", [D, FE])
    sw_up = inp("sw_up", [D, FE])
    sw_down = inp("sw_down", [FE, D])
    ident_bf = inp("ident_bf", [128, 128], BF16)
    ident_f32 = inp("ident_f32", [128, 128])
    ltri_strict = inp("ltri_strict", [128, 128])
    ltri32 = inp("ltri32", [32, 32])
    negmask = inp("negmask", [128, 8, 128])
    trimask_bf = inp("trimask_bf", [128, 128], BF16)
    iota16 = inp("iota16", [128, E])
    prevmask32 = inp("prevmask32", [NCORES, 32])
    tid_f32 = inp("tid_f32", [128, NT])
    ones_row = inp("ones_row", [1, 128])
    sel32 = inp("sel32", [128, NT * 32])
    sel1col = inp("sel1col", [32, NT * 128])

    out_x = nc.dram_tensor("out", [NTOK, D], F32, kind="ExternalOutput").ap()
    dmp = None
    if dump:
        dmp = nc.dram_tensor("d_stage", [NTOK, D], F32, kind="ExternalOutput").ap()

    xres = nc.dram_tensor("xres", [NTOK, D], F32).ap()
    beta_dram = nc.dram_tensor("beta_dram", [2, NH, 128], F32).ap()
    bl_dram = nc.dram_tensor("bl_dram", [2, 1, NH], F32).ap()
    agh_in = nc.dram_tensor("agh_in", [NTOK, D], BF16).ap()
    agh_out = nc.dram_tensor("agh_out", [NTOT, D], BF16,
                             addr_space="Shared" if SHARED_AG else "Local").ap()
    cnt_in = nc.dram_tensor("cnt_in", [1, E], F32).ap()
    cnt_out = nc.dram_tensor("cnt_out", [NCORES, E], F32).ap()
    slot_in = nc.dram_tensor("slot_in", [E * CAP, 1], F32).ap()
    rs_out = nc.dram_tensor("rs_out", [2 * CAP, 1], F32).ap()
    agy_in = nc.dram_tensor("agy_in", [2 * CAP, D], BF16).ap()
    agy_out = nc.dram_tensor("agy_out", [NCORES * 2 * CAP, D], BF16,
                             addr_space="Shared" if SHARED_AG else "Local").ap()
    shared_dram = nc.dram_tensor("shared_dram", [NTOK, D], BF16).ap()

    gp, sp = nc.gpsimd, nc.sync

    with contextlib.ExitStack() as st:
        ch = {
            "pe": Ch(st, nc, "c_pe"),
            "ve": Ch(st, nc, "c_ve"),
            "se": Ch(st, nc, "c_se"),
            "cc": Ch(st, nc, "c_cc"),
        }
        ve = SelfChain(nc.vector, ch["ve"], [ch["se"], ch["pe"]], _CHAINED_OPS)
        se = SelfChain(nc.scalar, ch["se"], [ch["ve"], ch["pe"]], _CHAINED_OPS)
        pe = SelfChain(nc.tensor, ch["pe"], [ch["ve"], ch["se"]], _PE_OPS, self_wait=False)
        comp = (ch["ve"], ch["se"], ch["pe"])
        spd = DmaChan(st, nc, "spd", 6, comp)
        gpd = DmaChan(st, nc, "gpd", 6, comp)
        sod = DmaChan(st, nc, "sod", 3, comp)
        semnums = ([c.sem.num for c in ch.values()]
                   + [sl.sem.num for d in (spd, gpd, sod) for sl in d.slots])
        lo, hi = min(semnums), max(semnums)

        def stage_boundary():
            nc.all_engine_barrier()

        nc.all_engine_barrier()
        gp.sem_clear(range(lo, hi + 1))
        nc.all_engine_barrier()

        const_bf = st.enter_context(nc.sbuf_tensor("k_ident_bf", [128, 128], BF16))
        identf_sb = st.enter_context(nc.sbuf_tensor("k_ident_f", [128, 128], F32))
        trim_sb = st.enter_context(nc.sbuf_tensor("k_trim", [128, 128], BF16))
        ltri_sb = st.enter_context(nc.sbuf_tensor("k_ltri", [128, 128], F32))
        ltri32_sb = st.enter_context(nc.sbuf_tensor("k_ltri32", [32, 32], F32))
        iota16_sb = st.enter_context(nc.sbuf_tensor("k_iota16", [128, E], F32))
        pm32_sb = st.enter_context(nc.sbuf_tensor("k_pm32", [NCORES, 32], F32))
        tid_sb = st.enter_context(nc.sbuf_tensor("k_tid", [128, NT], F32))
        ones_sb = st.enter_context(nc.sbuf_tensor("k_ones", [1, 128], F32))
        onescol_bf = st.enter_context(nc.sbuf_tensor("k_onescol", [128, 1], BF16))
        rb_sb = st.enter_context(nc.sbuf_tensor("k_rb", [128, E], F32))

        for dst, src in [(const_bf, ident_bf), (identf_sb, ident_f32), (trim_sb, trimask_bf),
                         (ltri_sb, ltri_strict), (ltri32_sb, ltri32), (iota16_sb, iota16),
                         (pm32_sb, prevmask32), (tid_sb, tid_f32), (ones_sb, ones_row),
                         (rb_sb, rb_bcast)]:
            spd.issue(sp, lambda dst=dst, src=src: sp.dma_start(out=dst[:], in_=src[:, :]))
        ve.memset(onescol_bf[:], 1.0)

        def ln_tile(xt, h1, sq, stats, eps, sub_mean, rr_out=None):
            s1, s2, s3 = stats[:, 0:1], stats[:, 1:2], stats[:, 2:3]
            ch["se"].inc(se.activation(sq[:], xt[:], AF.Square, accum_out=s1))
            if sub_mean:
                ve.tensor_reduce(s2, xt[:], axis=AX.X, op=ALU.add)
                ve.tensor_scalar(s2, s2, 1.0 / D, None, ALU.mult)
                ve.tensor_tensor(s3, s2, s2, ALU.mult)
                ch["se"].wait(ve)
                ve.tensor_scalar(s1, s1, 1.0 / D, s3, ALU.mult, ALU.subtract)
                ch["ve"].inc(ve.tensor_scalar(s1, s1, eps, None, ALU.add))
            else:
                ch["se"].wait(ve)
                ch["ve"].inc(ve.tensor_scalar(s1, s1, 1.0 / D, eps, ALU.mult, ALU.add))
            ch["ve"].wait(se)
            ch["se"].inc(se.activation(s1, s1, AF.Sqrt))
            ch["se"].wait(ve)
            ve.reciprocal(s1, s1)
            if rr_out is not None:
                ve.tensor_copy(rr_out, s1)
            if sub_mean:
                return ch["ve"].inc(ve.tensor_scalar(h1[:], xt[:], s2, s1, ALU.subtract, ALU.mult))
            return ch["ve"].inc(ve.tensor_scalar(h1[:], xt[:], s1, None, ALU.mult))

        with contextlib.ExitStack() as sa:
            t = lambda name, shape, dtype=F32: sa.enter_context(nc.sbuf_tensor(name, shape, dtype))
            win_sb = t("win_sb", [128, 8, ZXB], BF16)
            wout_sb = t("wout_sb", [128, 16, D], BF16)
            cw_sb = t("cw_sb", [128, 18, 4])
            cb_sb = t("cb_sb", [128, 18])
            dtb_sb = t("dtb_sb", [NH, 1])
            nega_sb = t("nega_sb", [NH, 1])
            dme_sb = t("dme_sb", [128, 16])
            mnw_sb = t("mnw_sb", [128, 16])
            ones32 = t("ones32", [NH, 128])
            xt2 = t("xt2", [128, 2, D])
            h1 = t("h1A", [128, D])
            sq = t("sqA", [128, D])
            stats = t("statsA", [128, 4])
            hb = t("hbA", [128, D], BF16)
            h_fm = t("h_fmA", [128, 8, 128], BF16)
            z_sil = t("z_sil", [128, 16, 128], BF16)
            xbc_raw = t("xbc_raw", [128, 18, 131], BF16)
            xc_bf = t("xc_bf", [128, 18, 128], BF16)
            acc_a = t("acc_a", [128, 9, 128], BF16)
            acc_b = t("acc_b", [128, 9, 128], BF16)
            cw_bf = t("cw_bf", [128, 18, 4], BF16)
            cb_bf = t("cb_bf", [128, 18], BF16)
            dt_raw = t("dt_raw", [NH, 128])
            dt_t0 = t("dt_t0", [NH, 128])
            dt_ab = t("dt_ab", [NH, 128])
            dt_sp2 = t("dt_sp2", [NH, 128])
            beta = t("beta", [NH, 128])
            gamma = t("gamma", [NH, 128])
            gT = t("gT", [128, NH])
            xm_tok = t("xm_tok", [128, 16, 128], BF16)
            b_tok = t("b_tok", [128, 128], BF16)
            bct = t("bct", [128, 128], BF16)
            negm_sb = t("negm_sb", [128, 8, 128])
            brep_all = t("brep_all", [128, 16, 128])
            argt = t("argA", [128, 8, 128])
            sp1 = t("sp1A", [128, 16, 128], BF16)
            sp2 = t("sp2A", [128, 16, 128], BF16)
            ebl = t("ebl", [128, NH])
            wtile = t("wtile", [128, NH])
            xw_all = t("xw_all", [128, 8, PH], BF16)
            t4_bf = t("t4_bf", [128, 4, 128], BF16)
            wt_bf = t("wt_bf", [128, NH], BF16)
            ebl_bf = t("ebl_bf", [128, NH], BF16)
            dme_bf = t("dme_bf", [128, 16], BF16)
            S0 = t("S0", [128, NH, PH], BF16)
            S1 = t("S1", [128, NH, PH], BF16)
            y_fm = t("y_fmA", [128, 16, 128], BF16)
            rstd_row = t("rstd_row", [1, 128])
            rstd_col = t("rstd_col", [128, 1])
            xnew = t("xnewA", [128, D])
            zt_bf = t("zt_bf", [128, 2, 512], BF16)
            dtf32 = t("dtf32", [128, 32])

            ps_zt = sa.enter_context(nc.psum_tensor("ps_zx", [128, 2, 512], F32))
            ps_out = sa.enter_context(nc.psum_tensor("ps_outA", [128, 2, 512], F32))
            ps_y = sa.enter_context(nc.psum_tensor("ps_yA", [128, 4, 128], F32))
            ps_S = sa.enter_context(nc.psum_tensor("ps_SA", [128, 8, PH], F32))
            ps_tr = sa.enter_context(nc.psum_tensor("ps_trA", [128, 8, 128], BF16))
            ps_m = sa.enter_context(nc.psum_tensor("ps_mA", [128, 128], F32))

            for k in range(8):
                gpd.issue(gp, lambda k=k: gp.dma_start(out=win_sb[:, k, :],
                                                       in_=w_in[k * 128:(k + 1) * 128, :]))
            for k in range(16):
                gpd.issue(gp, lambda k=k: gp.dma_start(out=wout_sb[:, k, :],
                                                       in_=w_out[k * 128:(k + 1) * 128, :]))
            for fc in range(18):
                spd.issue(sp, lambda fc=fc: sp.dma_start(out=cw_sb[:, fc, :],
                                                         in_=conv_w[fc * 128:(fc + 1) * 128, :]))
                spd.issue(sp, lambda fc=fc: sp.dma_start(out=cb_sb[:, fc:fc + 1],
                                                         in_=conv_b[fc * 128:(fc + 1) * 128, :]))
            spd.issue(sp, lambda: sp.dma_start(out=dtb_sb[:], in_=dt_bias[:, :]))
            spd.issue(sp, lambda: sp.dma_start(out=nega_sb[:], in_=a_log[:, :]))
            spd.issue(sp, lambda: sp.dma_start(out=dme_sb[:], in_=dm_exp[:, :]))
            spd.issue(sp, lambda: sp.dma_start(out=mnw_sb[:], in_=mnorm_w[:, :]))
            spd.issue(sp, lambda: sp.dma_start(out=negm_sb[:], in_=negmask[:, :, :]))
            spd.wait_all(ve)
            gpd.wait_all(ve)
            for k in range(16):
                ve.tensor_scalar(wout_sb[:, k, :], wout_sb[:, k, :], mnw_sb[:, k:k + 1], None, ALU.mult)
            ve.tensor_copy(dme_bf[:], dme_sb[:])
            ve.tensor_copy(cw_bf[:], cw_sb[:])
            ve.tensor_copy(cb_bf[:], cb_sb[:])
            ch["ve"].inc(ve.memset(ones32[:], 1.0))
            ch["ve"].wait(se)
            se.activation(nega_sb[:], nega_sb[:], AF.Exp)
            seA0 = ch["se"].inc(se.mul(nega_sb[:], nega_sb[:], -1.0))
            ch["ve"].inc(ve.memset(xbc_raw[:, :, 0:3], 0.0))

            xload = []
            xload.append(spd.issue(sp, lambda: sp.dma_start(out=xt2[:, 0, :], in_=x_in[0:128, :])))
            sadd_evac = [0] * 8
            xstore_tok = None
            yslot_evac = [0, 0, 0, 0]
            xnew_prev = 0

            for n in range(NT):
                seq, cc_ = divmod(n, TPS)
                buf = n % 2
                xtile = xt2[:, buf, :]
                par = n % 2
                if n + 1 < NT:
                    ch["ve"].wait(sp, xnew_prev)
                    xload.append(spd.issue(sp, lambda n=n: sp.dma_start(
                        out=xt2[:, (n + 1) % 2, :],
                        in_=x_in[(n + 1) * 128:(n + 2) * 128, :])))
                DmaChan.wait_tok(ve, xload[n])
                DmaChan.wait_tok(se, xload[n])

                ln_tile(xtile, h1, sq, stats, LN_EPS, True)
                ch["se"].inc(se.activation(sq[:], h1[:], AF.Square, accum_out=stats[:, 0:1]))
                ch["se"].wait(ve)
                ch["ve"].inc(ve.tensor_scalar(stats[:, 0:1], stats[:, 0:1], 1.0 / D, RMS_EPS,
                                              ALU.mult, ALU.add))
                ch["ve"].wait(se)
                ch["se"].inc(se.activation(stats[:, 0:1], stats[:, 0:1], AF.Sqrt))
                ch["se"].wait(ve)
                ve.reciprocal(stats[:, 0:1], stats[:, 0:1])
                ch["ve"].inc(ve.tensor_scalar(hb[:], h1[:], stats[:, 0:1], None, ALU.mult))
                hb_done = ch["ve"].n

                ch["ve"].wait(pe, hb_done)
                for k in range(8):
                    ch["pe"].inc(pe.transpose(ps_tr[:, k, :], hb[:, k * 128:(k + 1) * 128], const_bf[:]))
                ch["pe"].wait(ve)
                ch["ve"].inc(ve.tensor_copy(h_fm[:], ps_tr[:]))
                hfm_done = ch["ve"].n

                ch["ve"].wait(pe, hfm_done)
                for fg in range(9):
                    w = 512 if fg < 8 else 288
                    dst = ps_zt[:, fg % 2, 0:w]
                    for k in range(8):
                        pe.matmul(dst, h_fm[:, k, :], win_sb[:, k, fg * 512:fg * 512 + w],
                                  start=(k == 0), stop=(k == 7))
                    ve.tensor_copy(zt_bf[:, fg % 2, 0:w], dst)
                    base = (fg % 2) * 4
                    for j in range(4 if fg < 8 else 2):
                        pe.transpose(ps_tr[:, base + j, :],
                                     zt_bf[:, fg % 2, j * 128:(j + 1) * 128], const_bf[:])
                    if fg < 4:
                        se.activation(z_sil[:, fg * 4:(fg + 1) * 4, :],
                                      ps_tr[:, base:base + 4, :], AF.Silu)
                    elif fg < 8:
                        ve.tensor_copy(xbc_raw[:, fg * 4 - 16:fg * 4 - 12, 3:131],
                                       ps_tr[:, base:base + 4, :])
                    else:
                        ve.tensor_copy(xbc_raw[:, 16:18, 3:131], ps_tr[:, base:base + 2, :])
                        ve.tensor_copy(dtf32[:], ps_zt[:, fg % 2, 256:288])
                        pe.transpose(ps_m[0:NH, :], dtf32[:], identf_sb[:])
                        ve.tensor_copy(dt_raw[:], ps_m[0:NH, :])

                for h0 in (0, 9):
                    cw_b = lambda k: cw_bf[:, h0:h0 + 9, k:k + 1].broadcast_to([128, 9, 128])
                    xs = lambda k: xbc_raw[:, h0:h0 + 9, k:k + 128]
                    ve.tensor_tensor(acc_a[:], xs(0), cw_b(0), ALU.mult)
                    ve.tensor_tensor(acc_b[:], xs(1), cw_b(1), ALU.mult)
                    ve.tensor_tensor(acc_a[:], acc_a[:], acc_b[:], ALU.add)
                    ve.tensor_tensor(acc_b[:], xs(2), cw_b(2), ALU.mult)
                    ve.tensor_tensor(acc_a[:], acc_a[:], acc_b[:], ALU.add)
                    ve.tensor_tensor(acc_b[:], xs(3), cw_b(3), ALU.mult)
                    ve.tensor_tensor(acc_a[:], acc_a[:], acc_b[:], ALU.add)
                    ch["ve"].inc(ve.tensor_tensor(
                        acc_a[:], acc_a[:],
                        cb_bf[:, h0:h0 + 9].unsqueeze(2).broadcast_to([128, 9, 128]),
                        ALU.add))
                    ch["ve"].wait(se)
                    ch["se"].inc(se.activation(xc_bf[:, h0:h0 + 9, :], acc_a[:], AF.Silu))
                conv_done = ch["se"].n
                if cc_ == TPS - 1:
                    ch["ve"].inc(ve.memset(xbc_raw[:, :, 0:3], 0.0))
                else:
                    ch["ve"].inc(ve.tensor_copy(xbc_raw[:, :, 0:3], xbc_raw[:, :, 128:131]))

                ve.tensor_scalar(dt_t0[:], dt_raw[:], dtb_sb[:], None, ALU.add)
                ch["ve"].inc(ve.tensor_scalar(dt_ab[:], dt_t0[:], 1.0, None, ALU.mult))
                ch["ve"].wait(se)
                se.activation(dt_ab[:], dt_ab[:], AF.Abs)
                se.activation(dt_ab[:], dt_ab[:], AF.Exp, scale=-1.0)
                ch["se"].inc(se.activation(dt_ab[:], dt_ab[:], AF.Ln, bias=1.0))
                ch["se"].wait(ve, seA0)
                ch["se"].wait(ve)
                ve.tensor_scalar(dt_sp2[:], dt_t0[:], 0.0, None, ALU.max)
                ve.tensor_tensor(dt_sp2[:], dt_sp2[:], dt_ab[:], ALU.add)
                ve.tensor_scalar(dt_t0[:], dt_sp2[:], nega_sb[:], None, ALU.mult)
                ch["ve"].inc(ve.tensor_tensor_scan(beta[:], ones32[:], dt_t0[:], 0.0,
                                                   ALU.mult, ALU.add))
                beta_done = ch["ve"].n
                ch["ve"].wait(se)
                ch["se"].inc(se.activation(dt_ab[:], dt_sp2[:], AF.Ln))
                ch["se"].wait(ve)
                ch["ve"].inc(ve.tensor_tensor(gamma[:], beta[:], dt_ab[:], ALU.subtract))
                gamma_done = ch["ve"].n

                ch["ve"].wait(gp, beta_done)
                ch["se"].wait(gp)
                tok_bd = gpd.issue(gp, lambda: gp.dma_start(out=beta_dram[par, :, :], in_=beta[:, :]))
                tok_bl = gpd.issue(gp, lambda: gp.dma_start(out=bl_dram[par, :, :],
                                                            in_=beta[:, 127:128],
                                                            allow_slow_non_contiguous=True))
                DmaChan.wait_tok(gp, tok_bl)
                tok_blrep = gpd.issue(gp, lambda: gp.dma_start(
                    out=wtile[:],
                    in_=bl_dram[par, :, :].partition_broadcast(128)))
                DmaChan.wait_tok(se, tok_blrep)
                ch["se"].inc(se.activation(ebl[:], wtile[:], AF.Exp))
                ebl_done = ch["se"].n
                ve.tensor_copy(ebl_bf[:], ebl[:])
                DmaChan.wait_tok(gp, tok_bd)
                tok_ba = gpd.issue(gp, lambda: gp.dma_start(
                    out=brep_all[:].rearrange("p a b -> p (a b)"),
                    in_=beta_dram[par, 0:16, :].flatten().unsqueeze(0)
                        .partition_broadcast(128)))

                ch["ve"].wait(pe, gamma_done)
                ch["pe"].inc(pe.transpose(ps_m[:, 0:NH], gamma[:], identf_sb[0:NH, 0:NH]))
                ch["pe"].wait(ve)
                ch["ve"].inc(ve.tensor_copy(gT[:], ps_m[:, 0:NH]))
                ch["se"].wait(ve, ebl_done)
                DmaChan.wait_tok(ve, tok_blrep)
                ch["ve"].inc(ve.tensor_tensor(wtile[:], wtile[:], gT[:], ALU.subtract))
                ch["ve"].wait(se)
                ch["se"].inc(se.activation(wtile[:], wtile[:], AF.Exp))
                wtile_done = ch["se"].n
                ch["se"].wait(ve, wtile_done)
                ve.tensor_copy(wt_bf[:], wtile[:])

                ch["se"].wait(pe, conv_done)
                for base in range(0, 17, 8):
                    cnt = min(8, 17 - base)
                    for j in range(cnt):
                        ch["pe"].inc(pe.transpose(ps_tr[:, j, :], xc_bf[:, base + j, :],
                                                  const_bf[:]))
                    ch["pe"].wait(ve)
                    for j in range(cnt):
                        fc = base + j
                        dst2 = xm_tok[:, fc, :] if fc < 16 else b_tok[:]
                        ch["ve"].inc(ve.tensor_copy(dst2, ps_tr[:, j, :]))
                xmtok_done = ch["ve"].n

                ch["pe"].inc(pe.matmul(ps_m[:, 0:128], xc_bf[:, 16, :], xc_bf[:, 17, :],
                                       start=True, stop=True))
                ch["pe"].wait(ve)
                ch["ve"].inc(ve.tensor_copy(bct[:], ps_m[:, 0:128]))

                DmaChan.wait_tok(ve, tok_ba)
                DmaChan.wait_tok(se, tok_ba)
                for g in range(4):
                    e1t = sp1[:, 0:8, :]
                    g1 = sp1[:, 8:16, :]
                    e2b = sp2[:, 0:8, :]
                    g2 = sp2[:, 8:16, :]
                    if g == 2:
                        DmaChan.wait_tok(ve, tok_ba)
                        DmaChan.wait_tok(se, tok_ba)
                    ba_g = brep_all[:, (g % 2) * 8:(g % 2) * 8 + 8, :]
                    gt_rep = gT[:, g * 8:(g + 1) * 8].unsqueeze(2).broadcast_to([128, 8, 128])
                    ve.tensor_tensor(argt[:], ba_g, gt_rep, ALU.subtract)
                    ch["ve"].inc(ve.tensor_tensor(argt[:], argt[:], negm_sb[:], ALU.add))
                    ch["ve"].wait(se)
                    ch["se"].inc(se.activation(e1t, argt[:], AF.Exp))
                    ch["se"].inc(se.activation(e2b, ba_g, AF.Exp))
                    if g == 1:
                        tok_ba = gpd.issue(gp, lambda: gp.dma_start(
                            out=brep_all[:].rearrange("p a b -> p (a b)"),
                            in_=beta_dram[par, 16:32, :].flatten().unsqueeze(0)
                                .partition_broadcast(128)))
                    ch["se"].wait(ve)
                    bct_rep = bct[:, :].unsqueeze(1).broadcast_to([128, 8, 128])
                    ve.tensor_tensor(g1, e1t, bct_rep, ALU.mult)
                    c_rep = xc_bf[:, 17, :].unsqueeze(1).broadcast_to([128, 8, 128])
                    ch["ve"].inc(ve.tensor_tensor(g2, e2b, c_rep, ALU.mult))
                    g_done = ch["ve"].n

                    ch["ve"].wait(pe, g_done)
                    srcSg = (S0 if (n % 2 == 0) else S1)[:, g * 8:(g + 1) * 8, :]
                    dstSg = (S1 if (n % 2 == 0) else S0)[:, g * 8:(g + 1) * 8, :]
                    xm_g = xm_tok[:, g * 4:(g + 1) * 4, :].rearrange(
                        "p a (b c) -> p (a b) c", b=2)
                    ch["ve"].inc(ve.tensor_tensor(
                        xw_all[:], xm_g,
                        wt_bf[:, g * 8:(g + 1) * 8].unsqueeze(2).broadcast_to([128, 8, 64]),
                        ALU.mult))
                    ch["ve"].wait(pe)
                    for pr in range(4):
                        fcx = g * 4 + pr
                        for half in range(2):
                            h = 2 * fcx + half
                            hh = h - g * 8
                            lhs_xm = xm_tok[:, fcx, half * 64:half * 64 + 64]
                            tp = None if half == 0 else (0, 64)
                            if cc_ == 0:
                                ch["pe"].inc(pe.matmul(ps_y[half * 64:half * 64 + 64, pr, :],
                                                       lhs_xm, sp1[:, 8 + hh, :],
                                                       start=True, stop=True,
                                                       tile_position=tp))
                            else:
                                ch["pe"].inc(pe.matmul(ps_y[half * 64:half * 64 + 64, pr, :],
                                                       lhs_xm, sp1[:, 8 + hh, :],
                                                       start=True, stop=False,
                                                       tile_position=tp))
                                ch["pe"].inc(pe.matmul(ps_y[half * 64:half * 64 + 64, pr, :],
                                                       srcSg[:, hh, :], sp2[:, 8 + hh, :],
                                                       start=False, stop=True,
                                                       tile_position=tp))
                            ch["pe"].inc(pe.matmul(ps_S[:, hh, :], b_tok[:],
                                                   xw_all[:, hh, :],
                                                   start=True, stop=True))
                    allmm = ch["pe"].n
                    ch["pe"].wait(ve, allmm)
                    ch["se"].wait(ve, ebl_done)
                    if cc_ == 0:
                        ch["ve"].inc(ve.tensor_copy(dstSg, ps_S[:, 0:8, :]))
                    else:
                        ve.tensor_tensor(dstSg, srcSg,
                                         ebl_bf[:, g * 8:(g + 1) * 8].unsqueeze(2)
                                         .broadcast_to([128, 8, 64]), ALU.mult)
                        ch["ve"].inc(ve.tensor_tensor(dstSg, dstSg, ps_S[:, 0:8, :],
                                                      ALU.add))
                    ch["se"].wait(ve, conv_done)
                    ve.tensor_tensor(t4_bf[:], xc_bf[:, g * 4:(g + 1) * 4, :],
                                     dme_bf[:, g * 4:(g + 1) * 4].unsqueeze(2)
                                     .broadcast_to([128, 4, 128]), ALU.mult)
                    ch["ve"].inc(ve.tensor_tensor(y_fm[:, g * 4:(g + 1) * 4, :],
                                                  t4_bf[:], ps_y[:, :, :], ALU.add))

                ygb = sp1[:, :, :]
                ysq = sp2[:, :, :]
                ch["se"].wait(ve)
                ch["pe"].wait(ve)
                ve.tensor_tensor(ygb, y_fm[:], z_sil[:], ALU.mult)
                ch["ve"].inc(ve.tensor_tensor(ysq, ygb, ygb, ALU.mult))
                ysq_done = ch["ve"].n
                ch["ve"].wait(pe, ysq_done)
                for fcx in range(16):
                    ch["pe"].inc(pe.matmul(ps_m[0:1, 0:128], onescol_bf[:], sp2[:, fcx, :],
                                           start=(fcx == 0), stop=(fcx == 15)))
                ch["pe"].wait(ve)
                ve.tensor_scalar(rstd_row[:], ps_m[0:1, 0:128], 1.0 / DI, MN_EPS,
                                 ALU.mult, ALU.add)
                ch["ve"].inc(ve.tensor_scalar(rstd_row[:], rstd_row[:], 1.0, None, ALU.mult))
                ch["ve"].wait(se)
                ch["se"].inc(se.activation(rstd_row[:], rstd_row[:], AF.Sqrt))
                ch["se"].wait(ve)
                ch["ve"].inc(ve.reciprocal(rstd_row[:], rstd_row[:]))
                ch["ve"].wait(pe)
                ch["pe"].inc(pe.transpose(ps_m[:, 0:1], rstd_row[:], identf_sb[0:1, 0:1]))
                ch["pe"].wait(ve)
                ch["ve"].inc(ve.tensor_copy(rstd_col[:], ps_m[:, 0:1]))

                for half in range(2):
                    for k in range(16):
                        ch["pe"].inc(pe.matmul(ps_out[:, half, :], sp1[:, k, :],
                                               wout_sb[:, k, half * 512:(half + 1) * 512],
                                               start=(k == 0), stop=(k == 15)))
                ch["pe"].wait(ve)
                if xstore_tok is not None:
                    DmaChan.wait_tok(nc.vector, xstore_tok)
                ve.scalar_tensor_tensor(xnew[:, 0:512], ps_out[:, 0, :], rstd_col[:],
                                        h1[:, 0:512], ALU.mult, ALU.add)
                ch["ve"].inc(ve.scalar_tensor_tensor(xnew[:, 512:1024], ps_out[:, 1, :],
                                                     rstd_col[:], h1[:, 512:1024],
                                                     ALU.mult, ALU.add))
                xnew_prev = ch["ve"].n
                ch["ve"].wait(sp, xnew_prev)
                xstore_tok = spd.issue(sp, lambda n=n: sp.dma_start(
                    out=xres[n * 128:(n + 1) * 128, :], in_=xnew[:]))

            spd.wait_all(sp)

        stage_boundary()

        if (dump == 0 or dump >= 2) and B_PHASE > 0:
          with contextlib.ExitStack() as sb:
            t = lambda name, shape, dtype=F32: sb.enter_context(nc.sbuf_tensor(name, shape, dtype))
            wqkv_sb = t("wqkv_sb", [128, 8, D + 2 * HDA], BF16)
            wcp_sb = t("wcp_sb", [128, 8, D], BF16)
            qcw_sb = t("qcw_sb", [128, 3])
            qcb_sb = t("qcb_sb", [128, 1])
            kcw_sb = t("kcw_sb", [HDA, 3])
            kcb_sb = t("kcb_sb", [HDA, 1])
            vcw_sb = t("vcw_sb", [128, 3])
            vcb_sb = t("vcb_sb", [128, 1])
            xt2 = t("xt2B", [128, 2, D])
            h1 = t("h1B", [128, D])
            sq = t("sqB", [128, D])
            stats = t("statsB", [128, 4])
            hb = t("hbB", [128, D], BF16)
            h_fm = t("h_fmB", [128, 8, 128], BF16)
            q_raw = t("q_raw", [128, 8, T + 2], BF16)
            q_c = t("q_c", [128, 8, T], BF16)
            k_raw = t("k_raw", [HDA, T + 2], BF16)
            k_c = t("k_c", [HDA, T], BF16)
            k_dup = t("k_dup", [128, T], BF16)
            v_raw = t("v_raw", [128, T + 2], BF16)
            v_c = t("v_c", [128, T], BF16)
            v_tok = t("v_tok", [128, TPS, 72], BF16)
            acc_a = t("accBa", [128, 1024])
            acc_b = t("accBb", [128, 1024])
            pt = t("ptB", [128, 16, 128], BF16)
            y_acc = t("y_accB", [128, 16, 72])
            rec = t("recB", [128, 16])
            y_tok = t("y_tokB", [128, 16, 64], BF16)
            y_fmT = t("y_fmT", [128, 8, 128], BF16)
            xnew = t("xnewB", [128, D])

            for k in range(8):
                gpd.issue(gp, lambda k=k: gp.dma_start(out=wqkv_sb[:, k, :],
                                                       in_=w_qkv[k * 128:(k + 1) * 128, :]))
                gpd.issue(gp, lambda k=k: gp.dma_start(out=wcp_sb[:, k, :],
                                                       in_=w_cproj[k * 128:(k + 1) * 128, :]))
            spd.issue(sp, lambda: sp.dma_start(out=qcw_sb[:], in_=qconv_w[:, :]))
            spd.issue(sp, lambda: sp.dma_start(out=qcb_sb[:], in_=qconv_b[:, :]))
            spd.issue(sp, lambda: sp.dma_start(out=kcw_sb[:], in_=kconv_w[:, :]))
            spd.issue(sp, lambda: sp.dma_start(out=kcb_sb[:], in_=kconv_b[:, :]))
            spd.issue(sp, lambda: sp.dma_start(out=vcw_sb[64:128, :], in_=vconv_w[:, :]))
            spd.issue(sp, lambda: sp.dma_start(out=vcb_sb[64:128, :], in_=vconv_b[:, :]))
            spd.wait_all(nc.vector)
            spd.wait_all(nc.scalar)
            gpd.wait_all(nc.tensor)

            heads_eo = [0, 2, 4, 6, 8, 10, 12, 14, 1, 3, 5, 7, 9, 11, 13, 15]
            xstore_tok = None
            for s_ in range(BSEQ):
                with contextlib.ExitStack() as pp:
                    ps_pr = pp.enter_context(nc.psum_tensor(f"ps_prB{s_}", [128, 8, 128], F32))
                    ps_tr = pp.enter_context(nc.psum_tensor(f"ps_trB{s_}", [128, 8, 128], BF16))
                    pr_evac = [0] * 8
                    for cc_ in range(TPS):
                        n = s_ * TPS + cc_
                        tok = spd.issue(sp, lambda n=n: sp.dma_start(
                            out=xt2[:, n % 2, :], in_=xres[n * 128:(n + 1) * 128, :]))
                        DmaChan.wait_tok(nc.vector, tok)
                        DmaChan.wait_tok(nc.scalar, tok)
                        xtile = xt2[:, n % 2, :]
                        se.activation(sq[:], xtile, AF.Square, accum_out=stats[:, 0:1])
                        ve.tensor_scalar(stats[:, 0:1], stats[:, 0:1], 1.0 / D, RMS_EPS,
                                         ALU.mult, ALU.add)
                        se.activation(stats[:, 0:1], stats[:, 0:1], AF.Sqrt)
                        ve.reciprocal(stats[:, 0:1], stats[:, 0:1])
                        ve.tensor_scalar(hb[:], xtile, stats[:, 0:1], None, ALU.mult)
                        for k in range(8):
                            pe.transpose(ps_tr[:, k, :], hb[:, k * 128:(k + 1) * 128], const_bf[:])
                        ve.tensor_copy(h_fm[:], ps_tr[:])
                        for oc in range(9):
                            slot = oc % 8
                            dst = ps_pr[:, slot, :]
                            for k in range(8):
                                ch["pe"].inc(pe.matmul(dst, wqkv_sb[:, k, oc * 128:(oc + 1) * 128],
                                                       h_fm[:, k, :], start=(k == 0), stop=(k == 7)))
                            if oc < 8:
                                ve.tensor_copy(q_raw[:, oc, 2 + cc_ * 128:2 + (cc_ + 1) * 128], dst)
                            else:
                                ve.tensor_copy(k_raw[:, 2 + cc_ * 128:2 + (cc_ + 1) * 128],
                                               dst[0:64, :])
                                ve.tensor_copy(v_raw[64:128, 2 + cc_ * 128:2 + (cc_ + 1) * 128],
                                               dst[64:128, :])
                    ve.memset(q_raw[:, :, 0:2], 0.0)
                    ve.memset(k_raw[:, 0:2], 0.0)
                    ve.memset(v_raw[64:128, 0:2], 0.0)
                    HW_ = min(1024, T)
                    for fc in range(8):
                        for o in range(0, T, HW_):
                            ve.tensor_scalar(acc_a[:, 0:HW_], q_raw[:, fc, o:o + HW_],
                                             qcw_sb[:, 0:1], qcb_sb[:], ALU.mult, ALU.add)
                            ve.scalar_tensor_tensor(acc_b[:, 0:HW_], q_raw[:, fc, o + 1:o + 1 + HW_],
                                                    qcw_sb[:, 1:2], acc_a[:, 0:HW_], ALU.mult, ALU.add)
                            ve.scalar_tensor_tensor(q_c[:, fc, o:o + HW_],
                                                    q_raw[:, fc, o + 2:o + 2 + HW_],
                                                    qcw_sb[:, 2:3], acc_b[:, 0:HW_], ALU.mult, ALU.add)
                    for o in range(0, T, HW_):
                        ve.tensor_scalar(acc_a[0:64, 0:HW_], k_raw[:, o:o + HW_],
                                         kcw_sb[:, 0:1], kcb_sb[:], ALU.mult, ALU.add)
                        ve.scalar_tensor_tensor(acc_b[0:64, 0:HW_], k_raw[:, o + 1:o + 1 + HW_],
                                                kcw_sb[:, 1:2], acc_a[0:64, 0:HW_], ALU.mult, ALU.add)
                        ve.scalar_tensor_tensor(k_c[:, o:o + HW_], k_raw[:, o + 2:o + 2 + HW_],
                                                kcw_sb[:, 2:3], acc_b[0:64, 0:HW_], ALU.mult, ALU.add)
                        ve.tensor_scalar(acc_a[64:128, 0:HW_], v_raw[64:128, o:o + HW_],
                                         vcw_sb[64:128, 0:1], vcb_sb[64:128, :], ALU.mult, ALU.add)
                        ve.scalar_tensor_tensor(acc_b[64:128, 0:HW_], v_raw[64:128, o + 1:o + 1 + HW_],
                                                vcw_sb[64:128, 1:2], acc_a[64:128, 0:HW_], ALU.mult, ALU.add)
                        ve.scalar_tensor_tensor(v_c[64:128, o:o + HW_],
                                                v_raw[64:128, o + 2:o + 2 + HW_],
                                                vcw_sb[64:128, 2:3], acc_b[64:128, 0:HW_], ALU.mult, ALU.add)
                    kd_tok = spd.issue(sp, lambda: sp.dma_start(out=k_dup[64:128, :], in_=k_c[:, :]))
                    ve.tensor_copy(k_dup[0:64, :], k_c[:, :])
                    ve.memset(v_tok[:, :, 64:65], 1.0)
                    for sc8 in range(0, TPS, 8):
                        for j in range(8):
                            sc = sc8 + j
                            pe.transpose(ps_tr[:, j, 0:64],
                                         v_c[64:128, sc * 128:(sc + 1) * 128],
                                         const_bf[64:128, 64:128])
                        ve.tensor_copy(v_tok[:, sc8:sc8 + 8, 0:64], ps_tr[:, 0:8, 0:64])
                    DmaChan.wait_tok(nc.tensor, kd_tok)
                    DmaChan.wait_tok(nc.vector, kd_tok)

                if B_PHASE < 2:
                    continue
                with contextlib.ExitStack() as pa:
                    ps_st = pa.enter_context(nc.psum_tensor(f"ps_stB{s_}", [128, 4, 512], F32))
                    ps_pv = pa.enter_context(nc.psum_tensor(f"ps_pvB{s_}", [128, 8, 128], F32))
                    ps_tr2 = pa.enter_context(nc.psum_tensor(f"ps_tr2B{s_}", [128, 8, 128], BF16))
                    ps_o = ps_st
                    for cc_ in range(min(TPS, B_QC)):
                        n = s_ * TPS + cc_
                        rtok = spd.issue(sp, lambda n=n: sp.dma_start(
                            out=xt2[:, n % 2, :], in_=xres[n * 128:(n + 1) * 128, :]))
                        for sc in range(cc_ + 1):
                            for gi in range(4):
                                par_odd = gi >= 2
                                fc0 = (gi % 2) * 4
                                if not par_odd:
                                    lhs = k_c[:, sc * 128:(sc + 1) * 128]
                                    rhs = q_c[0:64, fc0:fc0 + 4, cc_ * 128:(cc_ + 1) * 128]
                                else:
                                    lhs = k_dup[64:128, sc * 128:(sc + 1) * 128]
                                    rhs = q_c[64:128, fc0:fc0 + 4, cc_ * 128:(cc_ + 1) * 128]
                                ch["pe"].inc(pe.matmul(ps_st[:, gi, :], lhs, rhs,
                                                       start=True, stop=True))
                            se.activation(pt[:].rearrange("p a b -> p (a b)"),
                                          ps_st[:].rearrange("p a b -> p (a b)"),
                                          AF.Exp, scale=0.125)
                            if sc == cc_:
                                tri_rep = trim_sb[:, :].unsqueeze(1).broadcast_to([128, 16, 128])
                                ve.tensor_tensor(pt[:], pt[:], tri_rep, ALU.mult)
                            for rnd in range(2):
                                for hj in range(8):
                                    hi = rnd * 8 + hj
                                    ch["pe"].inc(pe.matmul(ps_pv[:, hj, 0:65],
                                                           pt[:, hi, :], v_tok[:, sc, 0:65],
                                                           start=True, stop=True))
                                dstv = y_acc[:, rnd * 8:(rnd + 1) * 8, 0:65]
                                srcv = ps_pv[:, :, 0:65]
                                if sc == 0:
                                    ve.tensor_copy(dstv, srcv)
                                else:
                                    ve.tensor_tensor(dstv, dstv, srcv, ALU.add)
                        ve.reciprocal(rec[:, :].rearrange("p (a b) -> p a b", b=1), y_acc[:, :, 64:65])
                        rec_rep = rec[:, :].unsqueeze(2).broadcast_to([128, 16, 64])
                        ve.tensor_tensor(y_acc[:, :, 0:64], y_acc[:, :, 0:64], rec_rep, ALU.mult)
                        yv = y_tok[:].rearrange("p (a two) c -> p a two c", two=2)
                        ve.tensor_copy(yv[:, :, 0, :], y_acc[:, 0:8, 0:64])
                        ve.tensor_copy(yv[:, :, 1, :], y_acc[:, 8:16, 0:64])
                        for k in range(8):
                            pe.transpose(ps_tr2[:, k, :],
                                         y_tok[:, :, :].rearrange("p a b -> p (a b)")[:, k * 128:(k + 1) * 128],
                                         const_bf[:])
                        ve.tensor_copy(y_fmT[:], ps_tr2[:])
                        for half in range(2):
                            for k in range(8):
                                ch["pe"].inc(pe.matmul(ps_o[:, half, :], y_fmT[:, k, :],
                                                       wcp_sb[:, k, half * 512:(half + 1) * 512],
                                                       start=(k == 0), stop=(k == 7)))
                        if xstore_tok is not None:
                            DmaChan.wait_tok(nc.vector, xstore_tok)
                        DmaChan.wait_tok(nc.vector, rtok)
                        ve.tensor_tensor(xnew[:, 0:512], ps_o[:, 0, :], xt2[:, n % 2, 0:512], ALU.add)
                        ve.tensor_tensor(xnew[:, 512:1024], ps_o[:, 1, :], xt2[:, n % 2, 512:1024], ALU.add)
                        xstore_tok = spd.issue(sp, lambda n=n: sp.dma_start(
                            out=xres[n * 128:(n + 1) * 128, :], in_=xnew[:]))
            spd.wait_all(sp)
          stage_boundary()

        if dump == 0 or dump >= 3:
          with contextlib.ExitStack() as sc:
            t = lambda name, shape, dtype=F32: sc.enter_context(nc.sbuf_tensor(name, shape, dtype))
            wkey_sb = t("wkey_sb", [128, 8, FFN], BF16)
            wval_sb = t("wval_sb", [128, 32, D], BF16)
            wrec_sb = t("wrec_sb", [128, 8, D], BF16)
            maak_sb = t("maak_sb", [128, 8])
            maar_sb = t("maar_sb", [128, 8])
            xt2 = t("xt2C", [128, 2, D])
            x0t = t("x0tC", [128, 2, D])
            sq = t("sqC", [128, D])
            stats = t("statsC", [128, 4])
            hb = t("hbC", [128, D], BF16)
            h_fm = t("h_fmC", [128, 8, 128], BF16)
            hsh = t("hshC", [128, 8, 128], BF16)
            tailc = t("tailC", [128, 8, 1], BF16)
            xx = t("xxC", [128, 8, 128])
            xk_fm = t("xk_fmC", [128, 8, 128], BF16)
            xr_fm = t("xr_fmC", [128, 8, 128], BF16)
            kf = t("kfC", [128, 32, 128], BF16)
            tmqf = t("tmqfC", [128, 512])
            tmq = t("tmqC", [128, 512], BF16)
            sig = t("sigC", [128, D])
            t1 = t("t1C", [128, D])
            xnew = t("xnewC", [128, D])

            ps_kN = sc.enter_context(nc.psum_tensor("ps_kC", [128, 2, 512], F32))
            ps_kv = sc.enter_context(nc.psum_tensor("ps_kvC", [128, 2, 512], F32))
            ps_rec = sc.enter_context(nc.psum_tensor("ps_recC", [128, 2, 512], F32))
            ps_tr = sc.enter_context(nc.psum_tensor("ps_trC", [128, 8, 128], BF16))

            for k in range(8):
                gpd.issue(gp, lambda k=k: gp.dma_start(out=wkey_sb[:, k, :],
                                                       in_=w_key[k * 128:(k + 1) * 128, :]))
                gpd.issue(gp, lambda k=k: gp.dma_start(out=wrec_sb[:, k, :],
                                                       in_=w_rec[k * 128:(k + 1) * 128, :]))
            for k in range(32):
                gpd.issue(gp, lambda k=k: gp.dma_start(out=wval_sb[:, k, :],
                                                       in_=w_val[k * 128:(k + 1) * 128, :]))
            spd.issue(sp, lambda: sp.dma_start(out=maak_sb[:], in_=maa_k[:, :]))
            spd.issue(sp, lambda: sp.dma_start(out=maar_sb[:], in_=maa_r[:, :]))
            spd.wait_all(nc.vector)
            gpd.wait_all(nc.tensor)
            gpd.wait_all(nc.vector)

            xstore_tok = None
            for n in range(NT):
                s_, cc_ = divmod(n, TPS)
                rtok = spd.issue(sp, lambda n=n: sp.dma_start(
                    out=xt2[:, n % 2, :], in_=xres[n * 128:(n + 1) * 128, :]))
                rtok0 = spd.issue(sp, lambda n=n: sp.dma_start(
                    out=x0t[:, n % 2, :], in_=x_in[n * 128:(n + 1) * 128, :]))
                DmaChan.wait_tok(nc.vector, rtok)
                DmaChan.wait_tok(nc.scalar, rtok)
                xtile = xt2[:, n % 2, :]
                se.activation(sq[:], xtile, AF.Square, accum_out=stats[:, 0:1])
                ve.tensor_scalar(stats[:, 0:1], stats[:, 0:1], 1.0 / D, RMS_EPS, ALU.mult, ALU.add)
                se.activation(stats[:, 0:1], stats[:, 0:1], AF.Sqrt)
                ve.reciprocal(stats[:, 0:1], stats[:, 0:1])
                ve.tensor_scalar(hb[:], xtile, stats[:, 0:1], None, ALU.mult)
                for k in range(8):
                    pe.transpose(ps_tr[:, k, :], hb[:, k * 128:(k + 1) * 128], const_bf[:])
                ve.tensor_copy(h_fm[:], ps_tr[:])
                if cc_ == 0:
                    ve.memset(tailc[:], 0.0)
                ve.tensor_tensor(xx[:, :, 0:1], tailc[:], h_fm[:, :, 0:1], ALU.subtract)
                ve.tensor_tensor(xx[:, :, 1:128], h_fm[:, :, 0:127], h_fm[:, :, 1:128],
                                 ALU.subtract)
                ve.tensor_copy(tailc[:], h_fm[:, :, 127:128])
                maak_b = maak_sb[:].unsqueeze(2).broadcast_to([128, 8, 128])
                maar_b = maar_sb[:].unsqueeze(2).broadcast_to([128, 8, 128])
                ve.tensor_tensor(xk_fm[:], xx[:], maak_b, ALU.mult)
                ve.tensor_tensor(xk_fm[:], xk_fm[:], h_fm[:], ALU.add)
                ve.tensor_tensor(xr_fm[:], xx[:], maar_b, ALU.mult)
                ve.tensor_tensor(xr_fm[:], xr_fm[:], h_fm[:], ALU.add)
                for fg in range(8):
                    dst = ps_kN[:, fg % 2, :]
                    for k in range(8):
                        pe.matmul(dst, xk_fm[:, k, :],
                                  wkey_sb[:, k, fg * 512:(fg + 1) * 512],
                                  start=(k == 0), stop=(k == 7))
                    ve.tensor_scalar(tmqf[:], dst, 0.0, None, ALU.max)
                    ve.tensor_tensor(tmq[:], tmqf[:], tmqf[:], ALU.mult)
                    base = (fg % 2) * 4
                    for j in range(4):
                        pe.transpose(ps_tr[:, base + j, :],
                                     tmq[:, j * 128:(j + 1) * 128], const_bf[:])
                    ve.tensor_copy(kf[:, fg * 4:(fg + 1) * 4, :], ps_tr[:, base:base + 4, :])
                for half in range(2):
                    for kc in range(32):
                        pe.matmul(ps_kv[:, half, :], kf[:, kc, :],
                                  wval_sb[:, kc, half * 512:(half + 1) * 512],
                                  start=(kc == 0), stop=(kc == 31))
                for half in range(2):
                    for k in range(8):
                        pe.matmul(ps_rec[:, half, :], xr_fm[:, k, :],
                                  wrec_sb[:, k, half * 512:(half + 1) * 512],
                                  start=(k == 0), stop=(k == 7))
                se.activation(sig[:], ps_rec[:].rearrange("p a b -> p (a b)"), AF.Sigmoid)
                ve.tensor_tensor(t1[:], sig[:], ps_kv[:].rearrange("p a b -> p (a b)"), ALU.mult)
                if xstore_tok is not None:
                    DmaChan.wait_tok(nc.vector, xstore_tok)
                ve.tensor_tensor(xnew[:], t1[:], xtile, ALU.add)
                DmaChan.wait_tok(nc.vector, rtok0)
                ve.tensor_tensor(xnew[:], xnew[:], x0t[:, n % 2, :], ALU.add)
                xstore_tok = spd.issue(sp, lambda n=n: sp.dma_start(
                    out=xres[n * 128:(n + 1) * 128, :], in_=xnew[:]))
            spd.wait_all(sp)
          stage_boundary()

        if dump == 0 or dump >= 4:
          rg = [list(range(NCORES))]
          with contextlib.ExitStack() as sd_:
            t = lambda name, shape, dtype=F32: sd_.enter_context(nc.sbuf_tensor(name, shape, dtype))
            rw_sb = t("rw_sbD", [128, 8, E])
            ones_f = t("ones_fD", [128, 1])
            xt = t("xtD", [128, D])
            h1 = t("h1D", [128, D])
            sq = t("sqD", [128, D])
            stats = t("statsD", [128, 8])
            h1b = t("h1bD", [128, D], BF16)
            h1T = t("h1TD", [128, 8, 128])
            lg = t("lgD", [128, E])
            prb = t("prbD", [128, E])
            exl = t("exlD", [128, E])
            eq = t("eqD", [128, E])
            tt_ = t("ttD", [128, E])
            oh0 = t("oh0D", [128, E])
            oh1 = t("oh1D", [128, E])
            ohb = t("ohbD", [128, E])
            idxw_sb = t("idxwD", [128, NT, 4])
            ctile_sb = t("ctileD", [128, NT, E])
            tilesum_sb = t("tilesumD", [32, E])
            cnt_ag = t("cntagD", [NCORES, E])
            tb_sb = t("tbD", [32, E])
            tbrow = t("tbrowD", [1, E])
            cumf = t("cumfD", [128, E])
            rk = t("rkD", [128, 8])
            dest_u = t("destuD", [128, 2], U32)
            slots_sb = t("slotsD", [128, NT, 2])
            wgts_sb = t("wgtsD", [128, NT, 2])
            zz = t("zzD", [128, E * CAP // 128])
            sel32_sb = t("sel32D", [128, NT * 32])
            sel1_sb = t("sel1D", [32, NT * 128])

            routing_ps = contextlib.ExitStack()
            ps_r = routing_ps.enter_context(nc.psum_tensor("ps_rD", [128, 512], F32))
            ps_trD = routing_ps.enter_context(nc.psum_tensor("ps_trD", [128, 8, 128], F32))
            ps_ts = routing_ps.enter_context(nc.psum_tensor("ps_tsD", [32, E], F32))

            ve.memset(ones_f[:], 1.0)
            breg = gp.to_reg(E * CAP - 1)
            spd.issue(sp, lambda: sp.dma_start(out=sel32_sb[:], in_=sel32[:, :]))
            spd.issue(sp, lambda: sp.dma_start(out=sel1_sb[:], in_=sel1col[:, :]))
            for k in range(8):
                spd.issue(sp, lambda k=k: sp.dma_start(out=rw_sb[:, k, :],
                                                       in_=router_w[k * 128:(k + 1) * 128, :]))
            spd.wait_all(nc.tensor)
            spd.wait_all(nc.vector)
            ve.memset(zz[:], 0.0)
            ztok = gpd.issue(gp, lambda: gp.dma_start(
                out=slot_in.rearrange("(p n) o -> p (n o)", p=128), in_=zz[:]))
            zyb = t("zybD", [128, D], BF16)
            ve.memset(zyb[:], 0.0)
            for r0 in range(0, 2 * CAP, 128):
                gpd.issue(gp, lambda r0=r0: gp.dma_start(
                    out=agy_in[r0:r0 + 128, :], in_=zyb[:]))

            h1b_tok = None
            for n in range(NT):
                ltok = spd.issue(sp, lambda n=n: sp.dma_start(
                    out=xt[:], in_=xres[n * 128:(n + 1) * 128, :]))
                DmaChan.wait_tok(nc.vector, ltok)
                DmaChan.wait_tok(nc.scalar, ltok)
                ln_tile(xt, h1, sq, stats, LN_EPS, True)
                if h1b_tok is not None:
                    DmaChan.wait_tok(nc.vector, h1b_tok)
                ve.tensor_copy(h1b[:], h1[:])
                h1b_tok = spd.issue(sp, lambda n=n: sp.dma_start(
                    out=agh_in[n * 128:(n + 1) * 128, :], in_=h1b[:]))
                for k in range(8):
                    pe.transpose(ps_trD[:, k, :], h1[:, k * 128:(k + 1) * 128], identf_sb[:])
                for k in range(8):
                    ve.tensor_copy(h1T[:, k, :], ps_trD[:, k, :])
                for k in range(8):
                    pe.matmul(ps_r[:, 0:E], h1T[:, k, :], rw_sb[:, k, :],
                              start=(k == 0), stop=(k == 7))
                ve.tensor_tensor(lg[:], ps_r[:, 0:E], rb_sb[:], ALU.add)
                ve.reduce_max(stats[:, 0:1], lg[:], axis=AX.X)
                ve.tensor_scalar(stats[:, 0:1], stats[:, 0:1], -1.0, None, ALU.mult)
                se.activation(exl[:], lg[:], AF.Exp, bias=stats[:, 0:1],
                              accum_out=stats[:, 1:2])
                ve.reciprocal(stats[:, 1:2], stats[:, 1:2])
                ve.tensor_scalar(prb[:], exl[:], stats[:, 1:2], None, ALU.mult)
                ve.reduce_max(stats[:, 2:3], prb[:], axis=AX.X)
                ve.tensor_scalar(eq[:], prb[:], stats[:, 2:3], None, ALU.is_ge)
                ve.tensor_scalar(tt_[:], eq[:], -1e9, 1e9, ALU.mult, ALU.add)
                ve.tensor_tensor(tt_[:], tt_[:], iota16_sb[:], ALU.add)
                ve.tensor_reduce(idxw_sb[:, n, 0:1], tt_[:], axis=AX.X, op=ALU.min)
                ve.tensor_scalar(oh0[:], iota16_sb[:], idxw_sb[:, n, 0:1], None, ALU.is_equal)
                ve.tensor_tensor(tt_[:], prb[:], oh0[:], ALU.mult)
                ve.tensor_reduce(stats[:, 3:4], tt_[:], axis=AX.X, op=ALU.add)
                ve.scalar_tensor_tensor(tt_[:], oh0[:], -1e9, prb[:], ALU.mult, ALU.add)
                ve.reduce_max(stats[:, 4:5], tt_[:], axis=AX.X)
                ve.tensor_scalar(eq[:], tt_[:], stats[:, 4:5], None, ALU.is_ge)
                ve.tensor_scalar(eq[:], eq[:], -1e9, 1e9, ALU.mult, ALU.add)
                ve.tensor_tensor(eq[:], eq[:], iota16_sb[:], ALU.add)
                ve.tensor_reduce(idxw_sb[:, n, 1:2], eq[:], axis=AX.X, op=ALU.min)
                ve.tensor_scalar(oh1[:], iota16_sb[:], idxw_sb[:, n, 1:2], None, ALU.is_equal)
                ve.tensor_tensor(tt_[:], prb[:], oh1[:], ALU.mult)
                ve.tensor_reduce(stats[:, 5:6], tt_[:], axis=AX.X, op=ALU.add)
                ve.tensor_tensor(stats[:, 6:7], stats[:, 3:4], stats[:, 5:6], ALU.add)
                ve.tensor_scalar(stats[:, 6:7], stats[:, 6:7], 1e-9, None, ALU.add)
                ve.reciprocal(stats[:, 6:7], stats[:, 6:7])
                ve.tensor_tensor(idxw_sb[:, n, 2:3], stats[:, 3:4], stats[:, 6:7], ALU.mult)
                ve.tensor_tensor(idxw_sb[:, n, 3:4], stats[:, 5:6], stats[:, 6:7], ALU.mult)
                ve.tensor_tensor(ohb[:], oh0[:], oh1[:], ALU.add)
                pe.matmul(ps_r[:, 16:32], ltri_sb[:], ohb[:], start=True, stop=True)
                ve.tensor_copy(ctile_sb[:, n, :], ps_r[:, 16:32])
                pe.matmul(ps_ts[:, :], sel32_sb[:, n * 32:(n + 1) * 32], ohb[:],
                          start=(n == 0), stop=(n == NT - 1))

            ve.tensor_copy(tilesum_sb[:], ps_ts[:, :])
            pe.matmul(ps_r[0:1, 48:64], ones_f[0:32, :], tilesum_sb[:], start=True, stop=True)
            ve.tensor_copy(tbrow[:], ps_r[0:1, 48:64])
            ctok = gpd.issue(gp, lambda: gp.dma_start(out=cnt_in[:, :], in_=tbrow[:]))
            DmaChan.wait_tok(gp, ctok)
            ch["cc"].inc(gp.collective_compute(
                "AllGather", ALU.bypass, replica_groups=rg,
                ins=[cnt_in.opt()], outs=[cnt_out.opt()]))
            cc_cnt = ch["cc"].n
            spd.wait_all(gp)
            ch["cc"].inc(gp.collective_compute(
                "AllGather", ALU.bypass, replica_groups=rg,
                ins=[agh_in.opt()], outs=[agh_out.opt()]))
            cc_h = ch["cc"].n
            ch["cc"].wait(sp, cc_cnt)
            cgtok = spd.issue(sp, lambda: sp.dma_start(out=cnt_ag[:], in_=cnt_out[:, :]))
            DmaChan.wait_tok(nc.tensor, cgtok)
            pe.matmul(ps_r[0:32, 64:80], ltri32_sb[:], tilesum_sb[:], start=True, stop=False)
            pe.matmul(ps_r[0:32, 64:80], pm32_sb[:], cnt_ag[:], start=False, stop=True)
            ve.tensor_copy(tb_sb[:], ps_r[0:32, 64:80])

            sc_toks = []
            for n in range(NT):
                pe.matmul(ps_r[:, 80:96], sel1_sb[:, n * 128:(n + 1) * 128], tb_sb[:],
                          start=True, stop=True)
                ve.tensor_tensor(cumf[:], ctile_sb[:, n, :], ps_r[:, 80:96], ALU.add)
                for tkk in sc_toks:
                    DmaChan.wait_tok(nc.vector, tkk)
                sc_toks = []
                for kk in range(2):
                    ve.tensor_scalar(oh0[:], iota16_sb[:], idxw_sb[:, n, kk:kk + 1],
                                     None, ALU.is_equal)
                    ve.tensor_tensor(tt_[:], cumf[:], oh0[:], ALU.mult)
                    ve.tensor_reduce(rk[:, kk:kk + 1], tt_[:], axis=AX.X, op=ALU.add)
                    ve.tensor_scalar(rk[:, 2 + kk:3 + kk], rk[:, kk:kk + 1], float(CAP),
                                     None, ALU.is_lt)
                    ve.tensor_tensor(wgts_sb[:, n, kk:kk + 1], idxw_sb[:, n, 2 + kk:3 + kk],
                                     rk[:, 2 + kk:3 + kk], ALU.mult)
                    ve.tensor_scalar(rk[:, 4 + kk:5 + kk], rk[:, kk:kk + 1],
                                     float(CAP - 1), None, ALU.min)
                    ve.tensor_scalar(slots_sb[:, n, kk:kk + 1], idxw_sb[:, n, kk:kk + 1],
                                     float(CAP), rk[:, 4 + kk:5 + kk], ALU.mult, ALU.add)
                    ve.tensor_scalar(rk[:, 6 + kk:7 + kk], rk[:, 2 + kk:3 + kk],
                                     -1e6, 1e6, ALU.mult, ALU.add)
                    ve.tensor_scalar(rk[:, 4 + kk:5 + kk], idxw_sb[:, n, kk:kk + 1],
                                     float(CAP), rk[:, kk:kk + 1], ALU.mult, ALU.add)
                    ve.tensor_tensor(rk[:, 4 + kk:5 + kk], rk[:, 4 + kk:5 + kk],
                                     rk[:, 6 + kk:7 + kk], ALU.add)
                    ve.tensor_copy(dest_u[:, kk:kk + 1], rk[:, 4 + kk:5 + kk])
                DmaChan.wait_tok(gp, ztok)
                for kk in range(2):
                    sc_toks.append(gpd.issue(gp, lambda n=n, kk=kk: gp.indirect_dma_start(
                        out=slot_in,
                        out_offset=bass.IndirectOffsetOnAxis(ap=dest_u[:, kk:kk + 1], axis=0),
                        in_=tid_sb[:, n:n + 1], in_offset=None,
                        bounds_check=breg, oob_is_err=False)))

            routing_ps.close()
            gpd.wait_all(gp)
            ch["cc"].inc(gp.collective_compute(
                "ReduceScatter", ALU.add, replica_groups=rg,
                ins=[slot_in.opt()], outs=[rs_out.opt()]))
            cc_rs = ch["cc"].n

            with contextlib.ExitStack() as se_:
                t2 = lambda name, shape, dtype=F32: se_.enter_context(nc.sbuf_tensor(name, shape, dtype))
                wg_sb = t2("wg_sbD", [128, 8, FE], BF16)
                wu_sb = t2("wu_sbD", [128, 8, FE], BF16)
                wd_sb = t2("wd_sbD", [128, 11, D], BF16)
                tid_e = t2("tid_eD", [128, CAP // 128])
                tidu = t2("tiduD", [128, CAP // 128], U32)
                hrow = t2("hrowD", [128, 4, D], BF16)
                hfm_e = t2("hfmD", [128, 8, 512], BF16)
                sg = t2("sgD", [128, 512], BF16)
                gu_sb = t2("guD", [128, 11, 512], BF16)
                y_e = t2("y_eD", [128, 2, D], BF16)
                ps_g = se_.enter_context(nc.psum_tensor("ps_gD", [128, 512], F32))
                ps_u = se_.enter_context(nc.psum_tensor("ps_uD", [128, 512], F32))
                ps_d = se_.enter_context(nc.psum_tensor("ps_dD", [128, 2, 1024], F32))
                ps_t2 = se_.enter_context(nc.psum_tensor("ps_t2D", [128, 4, 128], BF16))

                NGRP = CAP // 512

                def load_shared_w():
                    for k in range(8):
                        gpd.issue(gp, lambda k=k: gp.dma_start(
                            out=wg_sb[:, k, :], in_=sw_gate[k * 128:(k + 1) * 128, :]))
                        gpd.issue(gp, lambda k=k: gp.dma_start(
                            out=wu_sb[:, k, :], in_=sw_up[k * 128:(k + 1) * 128, :]))
                    for k in range(11):
                        gpd.issue(gp, lambda k=k: gp.dma_start(
                            out=wd_sb[:, k, :], in_=sw_down[k * 128:(k + 1) * 128, :]))
                    gpd.wait_all(nc.tensor)
                    gpd.wait_all(nc.vector)

                sstore = []

                def shared_grp(grp):
                    htoks = []
                    for i in range(4):
                        r0 = grp * 512 + i * 128
                        htoks.append(spd.issue(sp, lambda r0=r0, i=i: sp.dma_start(
                            out=hrow[:, i, :], in_=agh_in[r0:r0 + 128, :])))
                    for ht in htoks:
                        DmaChan.wait_tok(nc.tensor, ht)
                    for i in range(4):
                        for k in range(8):
                            pe.transpose(ps_t2[:, k % 4, :], hrow[:, i, k * 128:(k + 1) * 128],
                                         const_bf[:])
                            ve.tensor_copy(hfm_e[:, k, i * 128:(i + 1) * 128],
                                           ps_t2[:, k % 4, :])
                    for fc in range(11):
                        for k in range(8):
                            pe.matmul(ps_g[:, :], wg_sb[:, k, fc * 128:(fc + 1) * 128],
                                      hfm_e[:, k, :], start=(k == 0), stop=(k == 7))
                            pe.matmul(ps_u[:, :], wu_sb[:, k, fc * 128:(fc + 1) * 128],
                                      hfm_e[:, k, :], start=(k == 0), stop=(k == 7))
                        se.activation(sg[:], ps_g[:, :], AF.Silu)
                        ve.tensor_tensor(gu_sb[:, fc, :], sg[:], ps_u[:, :], ALU.mult)
                    for i in range(4):
                        pd = ps_d[:, i % 2, :]
                        for kc in range(11):
                            for half in range(2):
                                pe.matmul(pd[:, half * 512:(half + 1) * 512],
                                          gu_sb[:, kc, i * 128:(i + 1) * 128],
                                          wd_sb[:, kc, half * 512:(half + 1) * 512],
                                          start=(kc == 0), stop=(kc == 10))
                        if len(sstore) >= 2:
                            DmaChan.wait_tok(nc.vector, sstore[-2])
                        ve.tensor_copy(y_e[:, i % 2, :], pd[:, :])
                        r0 = grp * 512 + i * 128
                        sstore.append(spd.issue(sp, lambda r0=r0, i=i: sp.dma_start(
                            out=shared_dram[r0:r0 + 128, :], in_=y_e[:, i % 2, :])))

                ch["cc"].wait(sp, cc_rs)
                ch["cc"].wait(gp, cc_h)
                ystore = []
                for ee in range(2):
                    for k in range(8):
                        gpd.issue(gp, lambda ee=ee, k=k: gp.dma_start(
                            out=wg_sb[:, k, :], in_=my_wg[ee, k * 128:(k + 1) * 128, :]))
                        gpd.issue(gp, lambda ee=ee, k=k: gp.dma_start(
                            out=wu_sb[:, k, :], in_=my_wu[ee, k * 128:(k + 1) * 128, :]))
                    for k in range(11):
                        gpd.issue(gp, lambda ee=ee, k=k: gp.dma_start(
                            out=wd_sb[:, k, :], in_=my_wd[ee, k * 128:(k + 1) * 128, :]))
                    ttok = spd.issue(sp, lambda ee=ee: sp.dma_start(
                        out=tid_e[:],
                        in_=rs_out[ee * CAP:(ee + 1) * CAP, :]
                            .rearrange("(p n) o -> p (n o)", p=128)))
                    DmaChan.wait_tok(nc.vector, ttok)
                    ve.tensor_copy(tidu[:], tid_e[:])
                    gpd.wait_all(nc.tensor)
                    gpd.wait_all(nc.vector)
                    for grp in range(NGRP):
                        gtoks = []
                        for i in range(4):
                            j = grp * 4 + i
                            gtoks.append(gpd.issue(gp, lambda i=i, j=j: gp.indirect_dma_start(
                                out=hrow[:, i, :], out_offset=None,
                                in_=agh_out,
                                in_offset=bass.IndirectOffsetOnAxis(ap=tidu[:, j:j + 1], axis=0))))
                        for gt in gtoks:
                            DmaChan.wait_tok(nc.tensor, gt)
                        for i in range(4):
                            for k in range(8):
                                pe.transpose(ps_t2[:, k % 4, :], hrow[:, i, k * 128:(k + 1) * 128],
                                             const_bf[:])
                                ve.tensor_copy(hfm_e[:, k, i * 128:(i + 1) * 128],
                                               ps_t2[:, k % 4, :])
                        for fc in range(11):
                            for k in range(8):
                                pe.matmul(ps_g[:, :], wg_sb[:, k, fc * 128:(fc + 1) * 128],
                                          hfm_e[:, k, :], start=(k == 0), stop=(k == 7))
                                pe.matmul(ps_u[:, :], wu_sb[:, k, fc * 128:(fc + 1) * 128],
                                          hfm_e[:, k, :], start=(k == 0), stop=(k == 7))
                            se.activation(sg[:], ps_g[:, :], AF.Silu)
                            ve.tensor_tensor(gu_sb[:, fc, :], sg[:], ps_u[:, :], ALU.mult)
                        for i in range(4):
                            pd = ps_d[:, i % 2, :]
                            for kc in range(11):
                                for half in range(2):
                                    pe.matmul(pd[:, half * 512:(half + 1) * 512],
                                              gu_sb[:, kc, i * 128:(i + 1) * 128],
                                              wd_sb[:, kc, half * 512:(half + 1) * 512],
                                              start=(kc == 0), stop=(kc == 10))
                            if len(ystore) >= 2:
                                DmaChan.wait_tok(nc.vector, ystore[-2])
                            ve.tensor_copy(y_e[:, i % 2, :], pd[:, :])
                            j = grp * 4 + i
                            row0 = ee * CAP + j
                            ystore.append(spd.issue(sp, lambda row0=row0, i=i: sp.dma_start(
                                out=agy_in[row0:row0 + (128 - 1) * (CAP // 128) + 1:CAP // 128, :],
                                in_=y_e[:, i % 2, :])))
                spd.wait_all(gp)
                gpd.wait_all(gp)
                ch["cc"].inc(gp.collective_compute(
                    "AllGather", ALU.bypass, replica_groups=rg,
                    ins=[agy_in.opt()], outs=[agy_out.opt()]))
                cc_y = ch["cc"].n

                load_shared_w()
                for grp in range(NTOK // 512):
                    shared_grp(grp)

            with contextlib.ExitStack() as sf_:
                t3 = lambda name, shape, dtype=F32: sf_.enter_context(nc.sbuf_tensor(name, shape, dtype))
                sh_b = t3("sh_bD", [128, D], BF16)
                y0t = t3("y0tD", [128, D], BF16)
                y1t = t3("y1tD", [128, D], BF16)
                su = t3("suD", [128, 2], U32)
                acc = t3("accD", [128, D])
                otok = None
                ch["cc"].wait(gp, cc_y)
                for n in range(NT):
                    ltok = spd.issue(sp, lambda n=n: sp.dma_start(
                        out=xt[:], in_=xres[n * 128:(n + 1) * 128, :]))
                    stok_ = spd.issue(sp, lambda n=n: sp.dma_start(
                        out=sh_b[:], in_=shared_dram[n * 128:(n + 1) * 128, :]))
                    if otok is not None:
                        DmaChan.wait_tok(nc.vector, otok)
                    ve.tensor_copy(su[:], slots_sb[:, n, :])
                    g0 = gpd.issue(gp, lambda: gp.indirect_dma_start(
                        out=y0t[:], out_offset=None, in_=agy_out,
                        in_offset=bass.IndirectOffsetOnAxis(ap=su[:, 0:1], axis=0)))
                    g1 = gpd.issue(gp, lambda: gp.indirect_dma_start(
                        out=y1t[:], out_offset=None, in_=agy_out,
                        in_offset=bass.IndirectOffsetOnAxis(ap=su[:, 1:2], axis=0)))
                    DmaChan.wait_tok(nc.vector, ltok)
                    DmaChan.wait_tok(nc.scalar, ltok)
                    ln_tile(xt, h1, sq, stats, LN_EPS, True)
                    DmaChan.wait_tok(nc.vector, g0)
                    ve.scalar_tensor_tensor(acc[:], y0t[:], wgts_sb[:, n, 0:1], xt[:],
                                            ALU.mult, ALU.add)
                    DmaChan.wait_tok(nc.vector, g1)
                    ve.scalar_tensor_tensor(acc[:], y1t[:], wgts_sb[:, n, 1:2], acc[:],
                                            ALU.mult, ALU.add)
                    ve.tensor_tensor(acc[:], acc[:], h1[:], ALU.add)
                    DmaChan.wait_tok(nc.vector, stok_)
                    ve.tensor_tensor(acc[:], acc[:], sh_b[:], ALU.add)
                    otok = sod.issue(sp, lambda n=n: sp.dma_start(
                        out=out_x[n * 128:(n + 1) * 128, :], in_=acc[:]))
                sod.wait_all(sp)
          stage_boundary()

        if dump in (1, 2, 3):
            with contextlib.ExitStack() as sd:
                tb = sd.enter_context(nc.sbuf_tensor("dumpbuf", [128, 2, D], F32))
                stok = [None, None]
                for n in range(NT):
                    b = n % 2
                    if stok[b] is not None:
                        DmaChan.wait_tok(sp, stok[b])
                    ltok = spd.issue(sp, lambda n=n, b=b: sp.dma_start(
                        out=tb[:, b, :], in_=xres[n * 128:(n + 1) * 128, :]))
                    DmaChan.wait_tok(sp, ltok)
                    stok[b] = sod.issue(sp, lambda n=n, b=b: sp.dma_start(
                        out=dmp[n * 128:(n + 1) * 128, :], in_=tb[:, b, :]))
                sod.wait_all(sp)
            stage_boundary()

        if dump:
            with contextlib.ExitStack() as so_:
                tb = so_.enter_context(nc.sbuf_tensor("outbuf", [128, 2, D], F32))
                stok = [None, None]
                for n in range(NT):
                    b = n % 2
                    if stok[b] is not None:
                        DmaChan.wait_tok(sp, stok[b])
                    ltok = spd.issue(sp, lambda n=n, b=b: sp.dma_start(
                        out=tb[:, b, :], in_=xres[n * 128:(n + 1) * 128, :]))
                    DmaChan.wait_tok(sp, ltok)
                    stok[b] = sod.issue(sp, lambda n=n, b=b: sp.dma_start(
                        out=out_x[n * 128:(n + 1) * 128, :], in_=tb[:, b, :]))
                sod.wait_all(sp)

    return nc


def make_consts(core_id):
    ident = np.eye(128, dtype=np.float32)
    ltri = np.triu(np.ones((128, 128), np.float32), 1)
    ltri32v = np.triu(np.ones((32, 32), np.float32), 1)
    nm = np.where(np.tril(np.ones((128, 128), np.float32)) > 0, 0.0, -1e30)
    nm = np.where(np.arange(128)[:, None] <= np.arange(128)[None, :], 0.0, -1e30).astype(np.float32)
    negmask_v = np.repeat(nm[:, None, :], 8, axis=1).copy()
    trim = (np.arange(128)[:, None] <= np.arange(128)[None, :]).astype(np.float32).astype(BFnp)
    iota16v = np.tile(np.arange(E, dtype=np.float32)[None, :], (128, 1))
    pm32 = np.zeros((NCORES, 32), np.float32)
    pm32[:core_id, :] = 1.0
    tid = (core_id * NTOK + (np.arange(NT)[None, :] * 128 + np.arange(128)[:, None])).astype(np.float32)
    sel32v = np.zeros((128, NT * 32), np.float32)
    for n in range(NT):
        sel32v[:, n * 32 + n] = 1.0
    sel1v = np.zeros((32, NT * 128), np.float32)
    for n in range(NT):
        sel1v[n, n * 128:(n + 1) * 128] = 1.0
    return {
        "sel32": sel32v,
        "sel1col": sel1v,
        "ident_bf": ident.astype(BFnp),
        "ident_f32": ident,
        "ltri_strict": ltri,
        "ltri32": ltri32v,
        "negmask": negmask_v,
        "trimask_bf": trim,
        "iota16": iota16v,
        "prevmask32": pm32,
        "tid_f32": tid,
        "ones_row": np.ones((1, 128), np.float32),
    }


def make_in_map(core_id, inputs):
    p = lambda k: np.ascontiguousarray(np.asarray(inputs[k], np.float32))
    m = {
        "x": np.ascontiguousarray(np.asarray(inputs["L0_attn_x"], np.float32)
                                  .reshape(-1, T, D)[BSEQ * core_id:BSEQ * (core_id + 1)]
                                  .reshape(NTOK, D)),
        "W_in": p("L0_attn_W_in"),
        "conv_w": p("L0_attn_conv_w"),
        "conv_b": p("L0_attn_conv_b").reshape(-1, 1),
        "A_log": p("L0_attn_A_log").reshape(-1, 1),
        "Dm_exp": np.ascontiguousarray(np.repeat(p("L0_attn_Dm"), PH).reshape(16, 128).T),
        "dt_bias": p("L0_attn_dt_bias").reshape(-1, 1),
        "mnorm_w": np.ascontiguousarray(p("L0_attn_mnorm_w").reshape(16, 128).T),
        "W_out": p("L0_attn_W_out"),
        "W_qkv": p("L0_attn_W_qkv"),
        "W_cproj": p("L0_attn_W_cproj"),
        "qconv_w": np.ascontiguousarray(np.tile(p("L0_attn_qconv_w"), (2, 1))),
        "qconv_b": np.ascontiguousarray(np.tile(p("L0_attn_qconv_b").reshape(-1, 1), (2, 1))),
        "kconv_w": p("L0_attn_kconv_w"),
        "kconv_b": p("L0_attn_kconv_b").reshape(-1, 1),
        "vconv_w": p("L0_attn_vconv_w"),
        "vconv_b": p("L0_attn_vconv_b").reshape(-1, 1),
        "time_maa_k": np.ascontiguousarray(p("L0_attn_time_maa_k").reshape(8, 128).T),
        "time_maa_r": np.ascontiguousarray(p("L0_attn_time_maa_r").reshape(8, 128).T),
        "W_key": p("L0_attn_W_key"),
        "W_rec": p("L0_attn_W_rec"),
        "W_val": p("L0_attn_W_val"),
        "router_w": p("L0_ffn_router_w"),
        "rb_bcast": np.ascontiguousarray(np.tile(p("L0_ffn_router_b").reshape(1, -1), (128, 1))),
        "my_wg": np.ascontiguousarray(p("L0_ffn_w_gate")[2 * core_id:2 * core_id + 2]),
        "my_wu": np.ascontiguousarray(p("L0_ffn_w_up")[2 * core_id:2 * core_id + 2]),
        "my_wd": np.ascontiguousarray(p("L0_ffn_w_down")[2 * core_id:2 * core_id + 2]),
        "sw_gate": p("L0_ffn_sw_gate"),
        "sw_up": p("L0_ffn_sw_up"),
        "sw_down": p("L0_ffn_sw_down"),
    }
    m.update(make_consts(core_id))
    return m


def kernel(**inputs):
    nc = build_nc()
    in_maps = [make_in_map(c, inputs) for c in range(NCORES)]
    res = run_bass_kernel_spmd(nc, in_maps, core_ids=list(range(NCORES)))
    out = np.concatenate([res.results[c]["out"].reshape(2, T, D) for c in range(NCORES)], axis=0)
    return out
